# version 05
# baseline (speedup 1.0000x reference)
import math
from contextlib import ExitStack

import numpy as np
import ml_dtypes

import concourse.bass as bass
import concourse.mybir as mybir
from concourse.bass_utils import run_bass_kernel_spmd

F32 = mybir.dt.float32
BF16 = mybir.dt.bfloat16
I32 = mybir.dt.int32
U16 = mybir.dt.uint16
AF = mybir.ActivationFunctionType
ALU = mybir.AluOpType
AX = mybir.AxisListType

S = 4096
D = 2048
H = 16
SEG = 512
NSEG = S // SEG
KC = D // 128
MLP = 16384
NEG = -1e30
EPS = 1e-5
ISQ = 1.0 / math.sqrt(128.0)
MEMW = 136

_G = [0, 1, 2, 3, 4, 6, 8, 12]
TAB32 = 0
for _j, _g in enumerate(_G):
    TAB32 |= _g << (4 * _j)
TAB32 -= 1 << 32

_DBG = {}


def _perm(n):
    q = np.empty(n, dtype=np.int64)
    for b in range(n // 256):
        base = 256 * b
        q[base:base + 128] = base + np.arange(0, 256, 2)
        q[base + 128:base + 256] = base + np.arange(1, 256, 2)
    return q


PERM_D = _perm(D)


def _host_prep(inputs):
    x = np.ascontiguousarray(np.asarray(inputs["L0_attn_x"], dtype=np.float32))
    wq = np.ascontiguousarray(np.asarray(inputs["L0_attn_Wq"], dtype=np.float32)[PERM_D, :])
    wk = np.ascontiguousarray(np.asarray(inputs["L0_attn_Wk"], dtype=np.float32)[PERM_D, :])
    wv = np.ascontiguousarray(np.asarray(inputs["L0_attn_Wv"], dtype=np.float32)[PERM_D, :])
    wo = np.ascontiguousarray(np.asarray(inputs["L0_attn_Wo"], dtype=np.float32))
    beta = np.ascontiguousarray(np.asarray(inputs["L0_attn_betas"], dtype=np.float32).reshape(H, 128))
    gw = np.ascontiguousarray(np.asarray(inputs["L0_ffn_gate_W_packed"], dtype=np.int32).T)
    uw = np.ascontiguousarray(np.asarray(inputs["L0_ffn_up_W_packed"], dtype=np.int32).T)
    dw = np.ascontiguousarray(np.asarray(inputs["L0_ffn_down_W_packed"], dtype=np.int32).T)
    gsc = np.ascontiguousarray(np.asarray(inputs["L0_ffn_gate_scales"], dtype=np.float32).T)
    usc = np.ascontiguousarray(np.asarray(inputs["L0_ffn_up_scales"], dtype=np.float32).T)
    dsc = np.ascontiguousarray(np.asarray(inputs["L0_ffn_down_scales"], dtype=np.float32).T)
    mask = np.where(np.arange(128)[None, :] <= np.arange(128)[:, None],
                    np.float32(0.0), np.float32(NEG)).astype(np.float32)
    idf = np.eye(128, dtype=np.float32)
    idb = np.eye(128, dtype=ml_dtypes.bfloat16)
    common = dict(wq=wq, wk=wk, wv=wv, wo=wo, beta=beta, gw=gw, uw=uw, dw=dw,
                  gsc=gsc, usc=usc, dsc=dsc, mask=mask, idf=idf, idb=idb)
    return [dict(common, x=np.ascontiguousarray(x[b])) for b in range(8)]


class Sem:
    __slots__ = ("h", "n", "name")

    def __init__(self, h, name):
        self.h = h
        self.n = 0
        self.name = name


class Plan:
    ENGS = ("sy", "sc", "dv", "pe", "po")

    def __init__(self, nc, es):
        self.nc = nc
        self.es = es
        self.rec = {k: [] for k in self.ENGS}
        self.wm = {}
        self.all_sems = []
        self.ticks = {}

    def sem(self, name):
        s = Sem(self.es.enter_context(self.nc.semaphore(name)), name)
        self.all_sems.append(s)
        return s

    def wait(self, ek, sem, val):
        if val is None or val <= 0:
            return
        key = (ek, sem.name)
        if self.wm.get(key, 0) >= val:
            return
        self.wm[key] = val
        self.rec[ek].append(lambda eng, h=sem.h, v=val: eng.wait_ge(h, v))

    def op(self, ek, f, tick=None, k=1):
        own = self.ticks.get(ek)
        if own is not None and own.n > 0:
            self.wait(ek, own, own.n)
        if tick is None:
            self.rec[ek].append(lambda eng, f=f: f(eng))
            return None
        tick.n += k

        def g(eng, f=f, h=tick.h, k=k):
            f(eng).then_inc(h, k)
        self.rec[ek].append(g)
        return tick.n

    def dma(self, ek, f, qsem):
        return self.op(ek, f, tick=qsem, k=16)


class Bank:

    def __init__(self, plan, handle, idx):
        self.p = plan
        self.h = handle
        self.idx = idx
        self.w = None
        self.readers = {}

    def ap(self):
        return self.h[:]

    def bf(self):
        return self.h[:].bitcast(BF16)

    def guard_write(self, ek="pe"):
        for s, v in self.readers.items():
            self.p.wait(ek, s, v)
        self.readers = {}

    def set_w(self, sem, val):
        self.w = (sem, val)

    def guard_read(self, ek):
        if self.w:
            self.p.wait(ek, self.w[0], self.w[1])

    def add_reader(self, sem, val):
        if val is not None:
            self.readers[sem] = max(self.readers.get(sem, 0), val)


class Buf:

    _n = [0]

    def __init__(self, plan, ap):
        self.p = plan
        self.t = ap
        self.w = None
        self.readers = {}
        self._qsem = None
        self._ssem = None

    def qsem(self):
        if self._qsem is None:
            Buf._n[0] += 1
            self._qsem = self.p.sem(f"bq{Buf._n[0]}")
        return self._qsem

    def ssem(self):
        if self._ssem is None:
            Buf._n[0] += 1
            self._ssem = self.p.sem(f"bs{Buf._n[0]}")
        return self._ssem

    def dload(self, ek, fn):
        self.wrg(ek)
        q = self.qsem()
        self.p.dma(ek, fn, q)
        self.set_w(q, q.n)
        return q.n

    def dspill(self, ek, fn):
        self.rd(ek)
        q = self.ssem()
        self.p.dma(ek, fn, q)
        self.add_rd(q, q.n)
        return q.n

    def rd(self, ek):
        if self.w:
            self.p.wait(ek, self.w[0], self.w[1])

    def add_rd(self, sem, val):
        if val is not None:
            self.readers[sem] = max(self.readers.get(sem, 0), val)

    def wrg(self, ek):
        for s, v in self.readers.items():
            self.p.wait(ek, s, v)
        self.readers = {}
        if self.w:
            self.p.wait(ek, self.w[0], self.w[1])

    def set_w(self, sem, val):
        self.w = (sem, val)


def build_program():
    taps = _DBG.get("taps", set())
    stop = _DBG.get("stop", "P4")
    ph_order = ["P1", "P2", "P4"]

    def want(ph):
        return ph_order.index(ph) <= ph_order.index(stop)

    nc = bass.Bass("TRN2", target_bir_lowering=False)
    es = ExitStack()
    p = Plan(nc, es)

    def dram(name, shape, dt, kind):
        if kind == "Internal" and name in taps:
            kind = "ExternalOutput"
        return nc.dram_tensor(name, shape, dt, kind=kind).ap()

    x_in = dram("x", [S, D], F32, "ExternalInput")
    w_in = {"q": dram("wq", [D, D], F32, "ExternalInput"),
            "k": dram("wk", [D, D], F32, "ExternalInput"),
            "v": dram("wv", [D, D], F32, "ExternalInput")}
    wo_in = dram("wo", [D, D], F32, "ExternalInput")
    beta_in = dram("beta", [H, 128], F32, "ExternalInput")
    gw_in = dram("gw", [1024, MLP], I32, "ExternalInput")
    uw_in = dram("uw", [1024, MLP], I32, "ExternalInput")
    dw_in = dram("dw", [MLP // 2, D], I32, "ExternalInput")
    gsc_in = dram("gsc", [64, MLP], F32, "ExternalInput")
    usc_in = dram("usc", [64, MLP], F32, "ExternalInput")
    dsc_in = dram("dsc", [512, D], F32, "ExternalInput")
    mask_in = dram("mask", [128, 128], F32, "ExternalInput")
    idf_in = dram("idf", [128, 128], F32, "ExternalInput")
    idb_in = dram("idb", [128, 128], BF16, "ExternalInput")
    out_d = dram("out", [S, D], F32, "ExternalOutput")

    qT_s = dram("qT_s", [D, S], BF16, "Internal")
    kT_s = dram("kT_s", [D, S], BF16, "Internal")
    v_s = dram("v_s", [S, D], BF16, "Internal")
    wo_s2 = dram("wo_s2", [16 * D, 128], BF16, "Internal")
    x2_s = dram("x2_s", [S, D], F32, "Internal")
    x2T_s = dram("x2T_s", [D, S], BF16, "Internal")
    wg_s = dram("wg_s", [D, MLP], BF16, "Internal")
    wu_s = dram("wu_s", [D, MLP], BF16, "Internal")
    wd_s = dram("wd_s", [MLP, D], BF16, "Internal")

    tPE = p.sem("tPE")
    tDV = p.sem("tDV")
    tAC = p.sem("tAC")
    tPO = p.sem("tPO")
    p.ticks = {"dv": tDV, "sc": tAC, "po": tPO}
    qCst = p.sem("qCst")
    qW = p.sem("qW")
    qX = p.sem("qX")
    qSeg = p.sem("qSeg")
    qPk = p.sem("qPk")
    qF = p.sem("qF")
    qSpA = p.sem("qSpA")
    qSpX = p.sem("qSpX")
    qSpW = p.sem("qSpW")
    qOut = p.sem("qOut")

    def sbuf(name, shape, dt):
        return es.enter_context(nc.sbuf_tensor(name, shape, dt))

    maskT = sbuf("maskT", [128, 128], F32)
    idfT = sbuf("idfT", [128, 128], F32)
    idbT = sbuf("idbT", [128, 128], BF16)
    betaT = sbuf("betaT", [H, 128], F32)
    btmp = sbuf("btmp", [H, 128], F32)
    bT = sbuf("bT", [128, H], F32)
    bTo = sbuf("bTo", [128, H], F32)
    onesb = sbuf("onesb", [128, 128], BF16)
    memf = sbuf("memf", [128, H * MEMW], F32)
    memb = sbuf("memb", [128, H * MEMW], BF16)
    zf = sbuf("zf", [128, H], F32)
    zrepb = sbuf("zrepb", [128, 128], BF16)
    epsT = sbuf("epsT", [128, 1], F32)
    negoneT = sbuf("negoneT", [128, 1], F32)
    nc.const_aps.register(F32, -1.0, negoneT[:])

    DQN = 512
    dq_in = [Buf(p, sbuf(f"dq_in{i}", [128, DQN], I32)) for i in range(2)]
    dq_sc = [Buf(p, sbuf(f"dq_sc{i}", [128, DQN], F32)) for i in range(2)]
    dq_t = {nm: Buf(p, sbuf(f"dq_{nm}", [128, DQN], I32))
            for nm in ("aL", "aH", "gL", "gH", "bL", "bH")}
    dq_gfL = Buf(p, sbuf("dq_gfL", [128, DQN], F32))
    dq_gfH = Buf(p, sbuf("dq_gfH", [128, DQN], F32))
    dq_m1L = Buf(p, sbuf("dq_m1L", [128, DQN], F32))
    dq_m1H = Buf(p, sbuf("dq_m1H", [128, DQN], F32))
    dq_tab = sbuf("dq_tab", [128, 1], I32)
    dq_wo = [Buf(p, sbuf(f"dq_wo{i}", [128, DQN], BF16)) for i in range(4)]

    banks = [Bank(p, es.enter_context(nc.psum_tensor(f"psb{i}", [128, 512], F32)), i)
             for i in range(8)]

    dq_specs = [(gw_in, gsc_in, wg_s, 8), (uw_in, usc_in, wu_s, 8),
                (dw_in, dsc_in, wd_s, 64)]
    dq_tiles = []
    for sgrp in range(8):
        for wi in (0, 1):
            for n4 in range(4):
                for kk in range(8):
                    dq_tiles.append((wi, kk, 2048 * sgrp + 512 * n4))
        for kk in range(8 * sgrp, 8 * sgrp + 8):
            for n4 in range(4):
                dq_tiles.append((2, kk, 512 * n4))
    NDQ = len(dq_tiles)
    dqs = {"dma": 0, "pool": 0, "dve": 0, "pool_t": [], "dve_t": [],
           "spill_tok": [], "group_tok": {}}

    def dq_issue_dma():
        i = dqs["dma"]
        if i >= NDQ:
            return True
        wi, kk, n0 = dq_tiles[i]
        pin, sin_, _, _ = dq_specs[wi]
        bi = i % 2
        dq_in[bi].dload("sy", lambda e, d=dq_in[bi].t,
                        s=pin[128 * kk:128 * (kk + 1), n0:n0 + DQN]:
                        e.dma_start(d[:], s))
        dq_sc[bi].dload("sy", lambda e, d=dq_sc[bi].t,
                        s=sin_[8 * kk:8 * (kk + 1), n0:n0 + DQN]
                        .unsqueeze(1).broadcast_to([8, 16, DQN]):
                        e.dma_start(d[:], s))
        dqs["dma"] = i + 1
        return True

    NSUBT = 1

    def dq_stageA(ti):
        while dqs["dma"] < ti + 1:
            dq_issue_dma()
        bi = ti % 2
        src_ = dq_in[bi].t[:]
        tb = dq_tab[:, 0:1].broadcast_to([128, DQN])
        dq_in[bi].rd("dv")
        for nm, f in (
            ("aL", lambda e, o, s: e.tensor_scalar(o[:], s, 2, 28, ALU.logical_shift_left, ALU.bitwise_and)),
            ("aH", lambda e, o, s: e.tensor_scalar(o[:], s, 2, 28, ALU.logical_shift_right, ALU.bitwise_and)),
            ("bL", lambda e, o, s: e.tensor_scalar(o[:], s, 8, None, ALU.bitwise_and)),
            ("bH", lambda e, o, s: e.tensor_scalar(o[:], s, 128, None, ALU.bitwise_and)),
        ):
            buf = dq_t[nm]
            buf.wrg("dv")
            p.op("dv", lambda e, o=buf.t, s=src_, f_=f: f_(e, o, s), tDV)
            buf.set_w(tDV, tDV.n)
        for gnm, anm in (("gL", "aL"), ("gH", "aH")):
            g, a = dq_t[gnm], dq_t[anm]
            g.wrg("dv")
            a.rd("dv")
            p.op("dv", lambda e, o=g.t, ai=a.t, t=tb: e.tensor_tensor(
                o[:], t, ai[:], ALU.logical_shift_right), tDV)
            g.set_w(tDV, tDV.n)
            a.add_rd(tDV, tDV.n)
            g.rd("dv")
            tgm = p.op("dv", lambda e, o=g.t: e.tensor_scalar(
                o[:], o[:], 15, None, ALU.bitwise_and), tDV)
            g.set_w(tDV, tgm)
        dq_in[bi].add_rd(tDV, tDV.n)
        for gnm, bnm, gf, m1, scl in (("gL", "bL", dq_gfL, dq_m1L, -0.25),
                                      ("gH", "bH", dq_gfH, dq_m1H, -0.015625)):
            g, b = dq_t[gnm], dq_t[bnm]
            g.rd("sc")
            gf.wrg("sc")
            t1 = p.op("sc", lambda e, o=gf.t, i=g.t: e.activation(o[:], i[:], AF.Copy), tAC)
            gf.set_w(tAC, t1)
            g.add_rd(tAC, t1)
            b.rd("sc")
            m1.wrg("sc")
            t2 = p.op("sc", lambda e, o=m1.t, i=b.t, s=scl: e.activation(
                o[:], i[:], AF.Copy, scale=s, bias=1.0), tAC)
            m1.set_w(tAC, t2)
            b.add_rd(tAC, t2)
        wi, kk, n0 = dq_tiles[ti]
        woL = dq_wo[2 * (ti % 2)]
        woH = dq_wo[2 * (ti % 2) + 1]
        dq_sc[bi].rd("po")
        for gf, m1, wo_ in ((dq_gfL, dq_m1L, woL), (dq_gfH, dq_m1H, woH)):
            gf.rd("po")
            p.op("po", lambda e, o=gf.t, b=dq_sc[bi].t: e.tensor_tensor(
                o[:], o[:], b[:], ALU.mult), tPO)
            gf.set_w(tPO, tPO.n)
            m1.rd("po")
            wo_.wrg("po")
            tw = p.op("po", lambda e, o=wo_.t, a=gf.t, b=m1.t: e.tensor_tensor(
                o[:], a[:], b[:], ALU.mult), tPO)
            wo_.set_w(tPO, tw)
            gf.add_rd(tPO, tw)
            m1.add_rd(tPO, tw)
        dq_sc[bi].add_rd(tPO, tPO.n)
        sp = dq_specs[wi][2]
        r0 = 256 * kk
        woL.dspill("sc", lambda e, d=sp[r0:r0 + 128, n0:n0 + DQN], s=woL.t:
                   e.dma_start(d, s[:]))
        woH.dspill("sc", lambda e, d=sp[r0 + 128:r0 + 256, n0:n0 + DQN], s=woH.t:
                   e.dma_start(d, s[:]))
        dqs["spill_tok"] = [(w._ssem, w._ssem.n) for w in dq_wo if w._ssem]

    def dq_dve_sub():
        si = dqs["dve"]
        if si >= NDQ:
            return False
        dq_stageA(si)
        dqs["dve"] = si + 1
        if (si + 1) % 96 == 0:
            dqs["group_tok"][(si + 1) // 96 - 1] = list(dqs["spill_tok"])
        return True

    def dq_pump(n):
        if _DBG.get("no_dq"):
            return
        for _ in range(n):
            if not dq_dve_sub():
                return

    def dq_group_spw(sgrp):
        return 96 * (sgrp + 1) * 2 * 16

    p.dma("sy", lambda e: e.dma_start(maskT[:], mask_in[:]), qCst)
    p.dma("sy", lambda e: e.dma_start(idfT[:], idf_in[:]), qCst)
    p.dma("sy", lambda e: e.dma_start(idbT[:], idb_in[:]), qCst)
    p.dma("sy", lambda e: e.dma_start(betaT[:], beta_in[:]), qCst)
    cst_n = qCst.n

    p.wait("dv", qCst, cst_n)
    p.op("dv", lambda e: e.memset(dq_tab[:], TAB32), tDV)
    p.op("dv", lambda e: e.memset(epsT[:], EPS), tDV)
    p.op("dv", lambda e: e.memset(negoneT[:], -1.0), tDV)
    p.op("dv", lambda e: e.memset(onesb[:], 1.0), tDV)
    p.op("dv", lambda e: e.memset(memf[:], 0.0), tDV)
    p.op("dv", lambda e: e.memset(memb[:], 0.0), tDV)
    p.op("dv", lambda e: e.memset(zf[:], 1.0 / 128.0), tDV)
    p.wait("dv", tDV, tDV.n)
    for h_ in range(H):
        t_z0 = p.op("dv", lambda e, c=MEMW * h_ + 128: e.memset(
            memb[:, c:c + 1], 1.0 / 128.0), tDV)
    t_dv_init = t_z0

    p.wait("sc", qCst, cst_n)
    t1 = p.op("sc", lambda e: e.activation(btmp[:], betaT[:], AF.Exp, scale=-1.0), tAC)
    p.wait("dv", tAC, t1)
    p.op("dv", lambda e: e.tensor_scalar(btmp[:], btmp[:], 1.0, None, ALU.add), tDV)
    p.wait("dv", tDV, tDV.n)
    t2 = p.op("dv", lambda e: e.reciprocal(btmp[:], btmp[:]), tDV)
    p.wait("pe", tDV, t2)
    p.wait("pe", qCst, cst_n)
    banks[0].guard_write("pe")
    t3 = p.op("pe", lambda e: e.transpose(banks[0].h[:, 0:H], btmp[:], idfT[0:H, 0:H]), tPE)
    banks[0].set_w(tPE, t3)
    banks[0].guard_read("dv")
    p.op("dv", lambda e: e.tensor_copy(bT[:], banks[0].h[:, 0:H]), tDV)
    p.wait("dv", tDV, tDV.n)
    t4 = p.op("dv", lambda e: e.tensor_scalar(bTo[:], bT[:], -1.0, 1.0, ALU.mult, ALU.add), tDV)
    banks[0].add_reader(tDV, t4)

    sc0 = ExitStack()

    def sb0(name, shape, dt):
        return sc0.enter_context(nc.sbuf_tensor(name, shape, dt))

    wstage = [Buf(p, sb0(f"wstage{i}", [128, 1024], F32)) for i in range(2)]
    xT = sb0("xT", [128, KC * S], BF16)
    xTb = Buf(p, xT)
    sc0b = ExitStack()
    xbst = [Buf(p, sc0b.enter_context(nc.sbuf_tensor(f"xbst{i}", [128, D], BF16)))
            for i in range(2)]
    xstage = wstage
    wbst = xbst

    for c in range(16):
        bo = c % 2
        for hlf in range(2):
            bi = hlf % 2
            wstage[bi].dload("sy", lambda e, d=wstage[bi].t,
                             s=wo_in[128 * c:128 * (c + 1), 1024 * hlf:1024 * (hlf + 1)]:
                             e.dma_start(d[:], s))
            wstage[bi].rd("sc")
            if hlf == 0:
                wbst[bo].wrg("sc")
            tt = p.op("sc", lambda e, o=wbst[bo].t[:, 1024 * hlf:1024 * (hlf + 1)],
                      i=wstage[bi].t: e.activation(o, i[:], AF.Copy), tAC)
            wstage[bi].add_rd(tAC, tt)
        wbst[bo].set_w(tAC, tt)
        dst = wo_s2.rearrange("(mc jv) m -> jv mc m", mc=16)[128 * c:128 * (c + 1)]
        wbst[bo].dspill("sc", lambda e, d=dst, s=wbst[bo].t:
                        e.dma_start(d, s[:].rearrange("q (mc m) -> q mc m", mc=16)))

    for tc in range(32):
        bo = tc % 2
        for hlf in range(2):
            bi = hlf % 2
            xstage[bi].dload("sy", lambda e, d=xstage[bi].t,
                             s=x_in[128 * tc:128 * (tc + 1), 1024 * hlf:1024 * (hlf + 1)]:
                             e.dma_start(d[:], s))
            xstage[bi].rd("sc")
            if hlf == 0:
                xbst[bo].wrg("sc")
            tt = p.op("sc", lambda e, o=xbst[bo].t[:, 1024 * hlf:1024 * (hlf + 1)],
                      i=xstage[bi].t: e.activation(o, i[:], AF.Copy), tAC)
            xstage[bi].add_rd(tAC, tt)
        xbst[bo].set_w(tAC, tt)
        xbst[bo].rd("pe")
        p.wait("pe", qCst, cst_n)
        last_ev = None
        for kc in range(KC):
            blk, eo = divmod(kc, 2)
            src = xbst[bo].t[:, 256 * blk + eo: 256 * (blk + 1): 2]
            bk = banks[kc % 2]
            bk.guard_write("pe")
            ttr = p.op("pe", lambda e, b=bk.bf(), s=src: e.transpose(
                b[:, 0:128], s, idbT[:]), tPE)
            bk.set_w(tPE, ttr)
            bk.guard_read("dv")
            ev = p.op("dv", lambda e, b=bk.bf(),
                      d=xT[:, kc * S + 128 * tc: kc * S + 128 * (tc + 1)]:
                      e.tensor_copy(d, b[:, 0:128]), tDV)
            bk.add_reader(tDV, ev)
            last_ev = ev
        xbst[bo].add_rd(tPE, tPE.n)
        dq_pump(1)
    xTb.set_w(tDV, tDV.n)

    wbst_for_gate = list(wbst)
    for ek in ("sy", "sc", "dv", "pe"):
        for sem in (tPE, tDV, tAC):
            p.wait(ek, sem, sem.n)
        for b_ in list(wstage) + list(xbst):
            for sm_ in (b_._qsem, b_._ssem):
                if sm_ is not None and sm_.n > 0:
                    p.wait(ek, sm_, sm_.n)
    sc0b.close()
    pjw = Buf(p, sb0("pjw", [128, KC * 512], BF16))
    pjo = Buf(p, sb0("pjo", [128, S // 2], BF16))
    vout = [Buf(p, sb0(f"vout{i}", [128, 512], BF16)) for i in range(2)]

    def load_wquarter(win, jq):
        last = None
        for kc in range(KC):
            bi = kc % 2
            wstage[bi].dload("sy", lambda e, d=wstage[bi].t[:, 0:512],
                             s=win[128 * kc:128 * (kc + 1), 512 * jq:512 * (jq + 1)]:
                             e.dma_start(d, s))
            wstage[bi].rd("sc")
            last = p.op("sc", lambda e, o=pjw.t[:, 512 * kc:512 * (kc + 1)],
                        i=wstage[bi].t[:, 0:512]: e.activation(o, i, AF.Copy), tAC)
            wstage[bi].add_rd(tAC, last)
        pjw.set_w(tAC, last)

    if want("P1"):
        xTb.rd("pe")
        for pj, dst_T in (("q", qT_s), ("k", kT_s)):
            for jq in range(_DBG.get("p1_jq", 4)):
                pjw.wrg("sc")
                load_wquarter(w_in[pj], jq)
                pjw.rd("pe")
                for j2 in range(4):
                    j = 4 * jq + j2
                    for half in range(2):
                      pjo.wrg("sc")
                      for tt8 in range(4 * half, 4 * half + 4):
                        bk = banks[2 + (tt8 % 2)]
                        bk.guard_write("pe")
                        for kc in range(KC):
                            p.op("pe", lambda e, b=bk.ap(),
                                 l=pjw.t[:, 512 * kc + 128 * j2: 512 * kc + 128 * (j2 + 1)],
                                 r=xT[:, kc * S + 512 * tt8: kc * S + 512 * (tt8 + 1)],
                                 st=(kc == 0), sp=(kc == KC - 1):
                                 e.matmul(b, l, r, start=st, stop=sp),
                                 tick=tPE if kc == KC - 1 else None)
                        bk.set_w(tPE, tPE.n)
                        bk.guard_read("sc")
                        ev = p.op("sc", lambda e, b=bk.ap(),
                                  d=pjo.t[:, 512 * (tt8 % 4):512 * (tt8 % 4) + 512]:
                                  e.activation(d, b, AF.Copy), tAC)
                        bk.add_reader(tAC, ev)
                      pjo.set_w(tAC, tAC.n)
                      pjo.dspill("sc", lambda e,
                                 d=dst_T[128 * j:128 * (j + 1), 2048 * half:2048 * (half + 1)],
                                 s=pjo.t: e.dma_start(d, s[:]))
                      dq_pump(2)
                pjw.add_rd(tPE, tPE.n)
        for jq in range(4):
            pjw.wrg("sc")
            load_wquarter(w_in["v"], jq)
            pjw.rd("pe")
            for tc in range(32):
                bk = banks[2 + (tc % 2)]
                bk.guard_write("pe")
                for kc in range(KC):
                    p.op("pe", lambda e, b=bk.ap(),
                         l=xT[:, kc * S + 128 * tc: kc * S + 128 * (tc + 1)],
                         r=pjw.t[:, 512 * kc:512 * (kc + 1)],
                         st=(kc == 0), sp=(kc == KC - 1):
                         e.matmul(b, l, r, start=st, stop=sp),
                         tick=tPE if kc == KC - 1 else None)
                bk.set_w(tPE, tPE.n)
                vo = vout[tc % 2]
                vo.wrg("sc")
                bk.guard_read("sc")
                ev = p.op("sc", lambda e, b=bk.ap(), d=vo.t:
                          e.activation(d[:], b, AF.Copy), tAC)
                bk.add_reader(tAC, ev)
                vo.set_w(tAC, ev)
                vo.dspill("sc", lambda e,
                          d=v_s[128 * tc:128 * (tc + 1), 512 * jq:512 * (jq + 1)],
                          s=vo.t: e.dma_start(d, s[:]))
                dq_pump(2)
            pjw.add_rd(tPE, tPE.n)

    dq_sem_names = set()
    for b_ in (dq_in + dq_sc + dq_wo + [dq_gfL, dq_gfH, dq_m1L, dq_m1H] +
               list(dq_t.values())):
        for sm_ in (b_._qsem, b_._ssem):
            if sm_ is not None:
                dq_sem_names.add(sm_.name)

    def barrier(name):
        for b_ in (dq_in + dq_sc + dq_wo):
            for sm_ in (b_._qsem, b_._ssem):
                if sm_ is not None:
                    dq_sem_names.add(sm_.name)
        for ek in ("sy", "sc", "dv", "pe"):
            for sem in p.all_sems:
                if sem.name in ("tPO",) or sem.name in dq_sem_names:
                    continue
                if sem.n > 0:
                    p.wait(ek, sem, sem.n)
    barrier("p1end")
    sc0.close()

    if want("P2"):
        sc2 = ExitStack()

        def sb2(name, shape, dt):
            return Buf(p, sc2.enter_context(nc.sbuf_tensor(name, shape, dt)))

        qq = sb2("qq", [128, KC * 512], BF16)
        kkb = sb2("kkb", [128, KC * 512], BF16)
        vv = sb2("vv", [128, 4 * D], BF16)
        xseg = sb2("xseg", [128, 4 * D], F32)
        wot = [sb2(f"wot{i}", [128, KC * 128], BF16) for i in range(2)]
        attT = sb2("attT", [128, H * 512], BF16)
        et = sb2("et", [128, 4 * 512], BF16)
        ptb = sb2("ptb", [128, 4 * 512], BF16)
        qsc = sb2("qsc", [128, 512], BF16)
        sq = sb2("sq", [128, 512], BF16)
        sk = sb2("sk", [128, 512], BF16)
        tmpA = sb2("tmpA", [128, 512], F32)
        tmpB = sb2("tmpB", [128, 512], F32)
        amT = sb2("amT", [128, 512], F32)
        rzq = sb2("rzq", [128, 512], F32)
        mxn = sb2("mxn", [128, 4], F32)
        smv = sb2("smv", [128, 4], F32)
        rsv = sb2("rsv", [128, 4], F32)
        rzk = sb2("rzk", [128, 4], F32)
        delta = sb2("delta", [128, 4 * 128], BF16)
        skn = sb2("skn", [128, 4 * 128], BF16)
        woTmp = sb2("woTmp", [128, 512], F32)
        nsum = sb2("nsum", [128, 4], F32)
        sqsum = sb2("sqsum", [128, 4], F32)
        varv = sb2("varv", [128, 4], F32)
        rstd = sb2("rstd", [128, 4], F32)
        m2t = sb2("m2t", [128, 4], F32)
        sqscr = sb2("sqscr", [128, D], BF16)
        x2b = sb2("x2b", [128, 4 * D], BF16)
        x2Tst = [sb2(f"x2Tst{i}", [128, 512], BF16) for i in range(2)]

        def load_seg(s):
            qq.dload("sy", lambda e, d=qq.t,
                     s_=qT_s[:, SEG * s:SEG * (s + 1)].rearrange("(kc q) n -> q kc n", q=128):
                     e.dma_start(d[:].rearrange("q (kc n) -> q kc n", kc=KC), s_))
            kkb.dload("sy", lambda e, d=kkb.t,
                      s_=kT_s[:, SEG * s:SEG * (s + 1)].rearrange("(kc q) n -> q kc n", q=128):
                      e.dma_start(d[:].rearrange("q (kc n) -> q kc n", kc=KC), s_))
            vv.dload("sy", lambda e, d=vv.t,
                     s_=v_s[SEG * s:SEG * (s + 1), :].rearrange("(c q) n -> q c n", q=128):
                     e.dma_start(d[:].rearrange("q (c n) -> q c n", c=4), s_))

        def load_xseg(s):
            xseg.dload("sy", lambda e, d=xseg.t,
                       s_=x_in[SEG * s:SEG * (s + 1), :].rearrange("(c q) n -> q c n", q=128):
                       e.dma_start(d[:].rearrange("q (c n) -> q c n", c=4), s_))

        if _DBG.get("nheads", H) < H:
            p.op("dv", lambda e: e.memset(attT.t[:], 0.0), tDV)
            attT.set_w(tDV, tDV.n)
        load_seg(0)
        load_xseg(0)
        mem_ready = [0] * H
        zf_ready = [0] * H
        zrep_rd = [None]

        for s in range(_DBG.get("nseg", NSEG)):
            for h in range(_DBG.get("nheads", H)):
                hs = 512 * h
                qq.rd("dv")
                qsc.wrg("dv")
                tq = p.op("dv", lambda e, o=qsc.t, i=qq.t[:, hs:hs + 512]:
                          e.tensor_scalar(o[:], i, ISQ, None, ALU.mult), tDV)
                qsc.set_w(tDV, tq)
                qq.add_rd(tDV, tq)
                sbanks = [(banks[0], 0), (banks[0], 128), (banks[1], 0), (banks[2], 0)]
                kkb.rd("pe")
                qsc.rd("pe")
                for c in range(4):
                    bk, off = sbanks[c]
                    w = 128 * (c + 1)
                    bk.guard_write("pe")
                    tm = p.op("pe", lambda e, b=bk.ap(), o=off, w_=w,
                              l=qsc.t[:, 128 * c:128 * (c + 1)],
                              r=kkb.t[:, hs:hs + w]:
                              e.matmul(b[:, o:o + w_], l, r, start=True, stop=True), tPE)
                    bk.set_w(tPE, tm)
                qsc.add_rd(tPE, tPE.n)
                kkb.add_rd(tPE, tPE.n)
                qq.rd("dv")
                tmpA.wrg("dv")
                p.op("dv", lambda e, i=qq.t[:, hs:hs + 512]: e.tensor_scalar(
                    tmpA.t[:], i, 0.0, None, ALU.min), tDV)
                tmpA.set_w(tDV, tDV.n)
                qq.add_rd(tDV, tDV.n)
                tmpA.rd("sc")
                tmpB.wrg("sc")
                tex = p.op("sc", lambda e: e.activation(tmpB.t[:], tmpA.t[:], AF.Exp), tAC)
                tmpB.set_w(tAC, tex)
                tmpA.add_rd(tAC, tex)
                for c in range(4):
                    bk, off = sbanks[c]
                    bk.guard_read("dv")
                    tmsk = p.op("dv", lambda e, b=bk.ap(), o=off, c_=c:
                                e.tensor_tensor(b[:, o + 128 * c_:o + 128 * (c_ + 1)],
                                                b[:, o + 128 * c_:o + 128 * (c_ + 1)],
                                                maskT[:], ALU.add), tDV)
                    bk.set_w(tDV, tmsk)
                tmpA.wrg("dv")
                p.op("dv", lambda e, i=qq.t[:, hs:hs + 512]: e.tensor_scalar(
                    tmpA.t[:], i, 0.0, None, ALU.max), tDV)
                tmpA.set_w(tDV, tDV.n)
                qq.add_rd(tDV, tDV.n)
                tmpA.rd("dv")
                tmpB.rd("dv")
                sq.wrg("dv")
                tsq = p.op("dv", lambda e: e.tensor_tensor(
                    sq.t[:], tmpB.t[:], tmpA.t[:], ALU.add), tDV)
                sq.set_w(tDV, tsq)
                tmpA.add_rd(tDV, tsq)
                tmpB.add_rd(tDV, tsq)
                for c in range(4):
                    bk, off = sbanks[c]
                    w = 128 * (c + 1)
                    bk.guard_read("dv")
                    tmx = p.op("dv", lambda e, b=bk.ap(), o=off, w_=w, c_=c:
                               e.tensor_reduce(mxn.t[:, c_:c_ + 1], b[:, o:o + w_],
                                               AX.X, ALU.max, negate=True), tDV)
                    mxn.set_w(tDV, tmx)
                    bk.add_reader(tDV, tmx)
                kkb.rd("dv")
                tmpA.wrg("dv")
                p.op("dv", lambda e, i=kkb.t[:, hs:hs + 512]: e.tensor_scalar(
                    tmpA.t[:], i, 0.0, None, ALU.min), tDV)
                tmpA.set_w(tDV, tDV.n)
                kkb.add_rd(tDV, tDV.n)
                tmpA.rd("sc")
                tmpB.wrg("sc")
                tex = p.op("sc", lambda e: e.activation(tmpB.t[:], tmpA.t[:], AF.Exp), tAC)
                tmpB.set_w(tAC, tex)
                tmpA.add_rd(tAC, tex)
                et.wrg("sc")
                for c in range(4):
                    bk, off = sbanks[c]
                    w = 128 * (c + 1)
                    bk.guard_read("sc")
                    p.wait("sc", tDV, mxn.w[1])
                    tex2 = p.op("sc", lambda e, b=bk.ap(), o=off, w_=w, c_=c:
                                e.activation(et.t[:, 512 * c_:512 * c_ + w_], b[:, o:o + w_],
                                             AF.Exp, bias=mxn.t[:, c_:c_ + 1],
                                             accum_out=smv.t[:, c_:c_ + 1]), tAC)
                    et.set_w(tAC, tex2)
                    smv.set_w(tAC, tex2)
                    bk.add_reader(tAC, tex2)
                tmpA.wrg("dv")
                p.op("dv", lambda e, i=kkb.t[:, hs:hs + 512]: e.tensor_scalar(
                    tmpA.t[:], i, 0.0, None, ALU.max), tDV)
                tmpA.set_w(tDV, tDV.n)
                kkb.add_rd(tDV, tDV.n)
                tmpA.rd("dv")
                tmpB.rd("dv")
                sk.wrg("dv")
                tsk = p.op("dv", lambda e: e.tensor_tensor(
                    sk.t[:], tmpB.t[:], tmpA.t[:], ALU.add), tDV)
                sk.set_w(tDV, tsk)
                tmpA.add_rd(tDV, tsk)
                tmpB.add_rd(tDV, tsk)
                bk5, bk6 = banks[5], banks[6]
                bk5.guard_write("pe")
                sq.rd("pe")
                p.wait("pe", tDV, t_dv_init)
                p.wait("pe", tAC, mem_ready[h])
                tm = p.op("pe", lambda e, b=bk5.ap(), h_=h:
                          e.matmul(b, memb[:, MEMW * h_:MEMW * h_ + 128], sq.t[:],
                                   start=True, stop=True), tPE)
                bk5.set_w(tPE, tm)
                p.wait("dv", tDV, zf_ready[h])
                if zrep_rd[0] is not None:
                    p.wait("dv", zrep_rd[0][0], zrep_rd[0][1])
                zrw = p.op("dv", lambda e, h_=h: e.tensor_scalar(
                    zrepb[:], onesb[:], zf[:, h_:h_ + 1], None, ALU.mult), tDV)
                bk6.guard_write("pe")
                p.wait("pe", tDV, zrw)
                tm6 = p.op("pe", lambda e, b=bk6.ap():
                           e.matmul(b, zrepb[:], sq.t[:], start=True, stop=True), tPE)
                bk6.set_w(tPE, tm6)
                zrep_rd[0] = (tPE, tm6)
                sq.add_rd(tPE, tPE.n)
                skm_pos = [(banks[7], 0), (banks[7], 132), (banks[7], 264), (banks[6], 0)]
                sk.rd("pe")
                for c in range(3):
                    bk, off = skm_pos[c]
                    bk.guard_write("pe")
                    tm = p.op("pe", lambda e, b=bk.ap(), o=off, c_=c, h_=h:
                              e.matmul(b[:, o:o + 132],
                                       sk.t[:, 128 * c_:128 * (c_ + 1)],
                                       memb[:, MEMW * h_:MEMW * h_ + 132],
                                       start=True, stop=True), tPE)
                    bk.set_w(tPE, tm)
                dq_pump(2)
                smv.rd("dv")
                rsv.wrg("dv")
                trs = p.op("dv", lambda e: e.reciprocal(rsv.t[:], smv.t[:]), tDV)
                rsv.set_w(tDV, trs)
                smv.add_rd(tDV, trs)
                et.rd("dv")
                rsv.rd("dv")
                for c in range(4):
                    w = 128 * (c + 1)
                    tp_ = p.op("dv", lambda e, c_=c, w_=w: e.tensor_scalar(
                        et.t[:, 512 * c_:512 * c_ + w_], et.t[:, 512 * c_:512 * c_ + w_],
                        rsv.t[:, c_:c_ + 1], None, ALU.mult), tDV)
                et.set_w(tDV, tp_)
                bk6.guard_read("dv")
                rzq.wrg("dv")
                trz = p.op("dv", lambda e, b=bk6.ap(): e.reciprocal(rzq.t[:], b), tDV)
                rzq.set_w(tDV, trz)
                bk6.add_reader(tDV, trz)
                bk5.guard_read("dv")
                rzq.rd("dv")
                amT.wrg("dv")
                tam = p.op("dv", lambda e, b=bk5.ap(): e.tensor_tensor(
                    amT.t[:], b, rzq.t[:], ALU.mult), tDV)
                amT.set_w(tDV, tam)
                bk5.add_reader(tDV, tam)
                rzq.add_rd(tDV, tam)
                bk, off = skm_pos[3]
                bk.guard_write("pe")
                tm = p.op("pe", lambda e, b=bk.ap(), o=off, h_=h:
                          e.matmul(b[:, o:o + 132],
                                   sk.t[:, 384:512],
                                   memb[:, MEMW * h_:MEMW * h_ + 132],
                                   start=True, stop=True), tPE)
                bk.set_w(tPE, tm)
                sk.add_rd(tPE, tPE.n)
                et.rd("pe")
                for j in range(4):
                    bk = banks[3]
                    bk.guard_write("pe")
                    for c in range(j, 4):
                        tt = p.op("pe", lambda e, b=bk.bf(), c_=c, j_=j:
                                  e.transpose(b[:, 128 * c_:128 * (c_ + 1)],
                                              et.t[:, 512 * c_ + 128 * j_:512 * c_ + 128 * (j_ + 1)],
                                              idbT[:]), tPE)
                    bk.set_w(tPE, tPE.n)
                    bk.guard_read("dv")
                    ptb.wrg("dv") if j == 0 else None
                    ev = p.op("dv", lambda e, b=bk.bf(), j_=j:
                              e.tensor_copy(ptb.t[:, 512 * j_ + 128 * j_:512 * j_ + 512],
                                            b[:, 128 * j_:512]), tDV)
                    bk.add_reader(tDV, ev)
                et.add_rd(tPE, tPE.n)
                ptb.set_w(tDV, tDV.n)
                bk4 = banks[4]
                bk4.guard_write("pe")
                ptb.rd("pe")
                vv.rd("pe")
                for j in range(4):
                    tm = p.op("pe", lambda e, b=bk4.ap(), j_=j, h_=h:
                              e.matmul(b[:, 128 * j_:512],
                                       vv.t[:, D * j_ + 128 * h_:D * j_ + 128 * (h_ + 1)],
                                       ptb.t[:, 512 * j_ + 128 * j_:512 * j_ + 512],
                                       start=(j_ == 0), stop=(j_ == 3)), tPE)
                bk4.set_w(tPE, tPE.n)
                ptb.add_rd(tPE, tPE.n)
                vv.add_rd(tPE, tPE.n)
                rzk.wrg("dv")
                delta.wrg("dv")
                vv.rd("dv")
                tlast = None
                for c in range(4):
                    bk, off = skm_pos[c]
                    bk.guard_read("dv")
                    p.op("dv", lambda e, b=bk.ap(), o=off, c_=c:
                         e.reciprocal(rzk.t[:, c_:c_ + 1], b[:, o + 128:o + 129]), tDV)
                    tlast = p.op("dv", lambda e, b=bk.ap(), o=off, c_=c, h_=h:
                                 e.scalar_tensor_tensor(
                                     delta.t[:, 128 * c_:128 * (c_ + 1)],
                                     b[:, o:o + 128], rzk.t[:, c_:c_ + 1],
                                     vv.t[:, D * c_ + 128 * h_:D * c_ + 128 * (h_ + 1)],
                                     ALU.mult, ALU.subtract), tDV)
                    bk.add_reader(tDV, tlast)
                rzk.set_w(tDV, tlast)
                delta.set_w(tDV, tlast)
                vv.add_rd(tDV, tlast)
                sk.rd("pe")
                skn.wrg("dv")
                for c in range(4):
                    bk = banks[3]
                    bk.guard_write("pe")
                    tt = p.op("pe", lambda e, b=bk.bf(), c_=c:
                              e.transpose(b[:, 0:128],
                                          sk.t[:, 128 * c_:128 * (c_ + 1)], idbT[:]), tPE)
                    bk.set_w(tPE, tt)
                    bk.guard_read("dv")
                    ev = p.op("dv", lambda e, b=bk.bf(), c_=c:
                              e.tensor_copy(skn.t[:, 128 * c_:128 * (c_ + 1)],
                                            b[:, 0:128]), tDV)
                    bk.add_reader(tDV, ev)
                sk.add_rd(tPE, tPE.n)
                skn.set_w(tDV, tDV.n)
                bk3 = banks[3]
                bk3.guard_write("pe")
                skn.rd("pe")
                delta.rd("pe")
                for c in range(4):
                    tm = p.op("pe", lambda e, b=bk3.ap(), c_=c:
                              e.matmul(b[:, 0:128],
                                       skn.t[:, 128 * c_:128 * (c_ + 1)],
                                       delta.t[:, 128 * c_:128 * (c_ + 1)],
                                       start=(c_ == 0), stop=(c_ == 3)), tPE)
                bk3.set_w(tPE, tPE.n)
                skn.add_rd(tPE, tPE.n)
                delta.add_rd(tPE, tPE.n)
                bk3.guard_read("dv")
                tmu = p.op("dv", lambda e, b=bk3.ap(), h_=h: e.tensor_tensor(
                    memf[:, MEMW * h_:MEMW * h_ + 128],
                    memf[:, MEMW * h_:MEMW * h_ + 128], b[:, 0:128], ALU.subtract), tDV)
                bk3.add_reader(tDV, tmu)
                sk.rd("dv")
                tzs = p.op("dv", lambda e, h_=h: e.tensor_reduce(
                    tmpA.t[:, 0:1], sk.t[:], AX.X, ALU.add), tDV)
                tmpA.set_w(tDV, tzs)
                sk.add_rd(tDV, tzs)
                tzf = p.op("dv", lambda e, h_=h: e.tensor_tensor(
                    zf[:, h_:h_ + 1], zf[:, h_:h_ + 1], tmpA.t[:, 0:1], ALU.add), tDV)
                tmpA.add_rd(tDV, tzf)
                p.wait("sc", tDV, tmu)
                tmb = p.op("sc", lambda e, h_=h: e.activation(
                    memb[:, MEMW * h_:MEMW * h_ + 128],
                    memf[:, MEMW * h_:MEMW * h_ + 128], AF.Copy), tAC)
                p.wait("sc", tDV, tzf)
                tmz = p.op("sc", lambda e, h_=h: e.activation(
                    memb[:, MEMW * h_ + 128:MEMW * h_ + 129],
                    zf[:, h_:h_ + 1], AF.Copy), tAC)
                mem_ready[h] = tmz
                zf_ready[h] = tzf
                amT.rd("dv")
                tmpB.wrg("dv")
                tb1 = p.op("dv", lambda e, h_=h: e.tensor_scalar(
                    tmpB.t[:], amT.t[:], bT[:, h_:h_ + 1], None, ALU.mult), tDV)
                tmpB.set_w(tDV, tb1)
                amT.add_rd(tDV, tb1)
                bk4.guard_read("dv")
                tmpB.rd("dv")
                tat = p.op("dv", lambda e, b=bk4.ap(), h_=h: e.scalar_tensor_tensor(
                    attT.t[:, 512 * h_:512 * (h_ + 1)], b,
                    bTo[:, h_:h_ + 1], tmpB.t[:], ALU.mult, ALU.add), tDV)
                bk4.add_reader(tDV, tat)
                tmpB.add_rd(tDV, tat)
                attT.set_w(tDV, tat)
                dq_pump(1)
            if s + 1 < _DBG.get("nseg", NSEG):
                load_seg(s + 1)
            attT.rd("pe")
            xseg.rd("dv")
            for mc in range(16):
                wt = wot[mc % 2]
                if mc == 0 and s == 0:
                    for wb in wbst_for_gate:
                        if wb._ssem is not None:
                            p.wait("sy", wb._ssem, wb._ssem.n)
                wt.dload("sy", lambda e, d=wt.t, mc_=mc:
                         e.dma_start(d[:].rearrange("q (kc m) -> q kc m", kc=KC),
                                     wo_s2[2048 * mc_:2048 * (mc_ + 1), :]
                                     .rearrange("(kc q) m -> q kc m", q=128)))
                bkw = banks[4 + (mc % 2)]
                bkw.guard_write("pe")
                wt.rd("pe")
                attR = attT.t[:].rearrange("p (a b c) -> p a b c", a=16, b=32)
                for jv in range(16):
                    tm = p.op("pe", lambda e, b=bkw.ap(), jv_=jv, w_=wt.t,
                              r_=attR[:, :, :, jv]:
                              e.matmul(b, w_[:, 128 * jv_:128 * (jv_ + 1)],
                                       r_, start=(jv_ == 0), stop=(jv_ == 15)), tPE)
                bkw.set_w(tPE, tPE.n)
                wt.add_rd(tPE, tPE.n)
                bkw.guard_read("dv")
                woTmp.wrg("dv")
                tev = p.op("dv", lambda e, b=bkw.ap(): e.tensor_copy(woTmp.t[:], b), tDV)
                woTmp.set_w(tDV, tev)
                bkw.add_reader(tDV, tev)
                woTmp.rd("pe")
                for c in range(4):
                    bkt = banks[6 + (c % 2)]
                    bkt.guard_write("pe")
                    tt = p.op("pe", lambda e, b=bkt.ap(), c_=c:
                              e.transpose(b[:, 0:128],
                                          woTmp.t[:, 128 * c_:128 * (c_ + 1)], idfT[:]), tPE)
                    bkt.set_w(tPE, tt)
                    bkt.guard_read("dv")
                    tev2 = p.op("dv", lambda e, b=bkt.ap(), c_=c, mc_=mc:
                                e.tensor_tensor(xseg.t[:, D * c_ + 128 * mc_:D * c_ + 128 * (mc_ + 1)],
                                                xseg.t[:, D * c_ + 128 * mc_:D * c_ + 128 * (mc_ + 1)],
                                                b[:, 0:128], ALU.add), tDV)
                    bkt.add_reader(tDV, tev2)
                woTmp.add_rd(tPE, tPE.n)
                xseg.set_w(tDV, tDV.n)
            xseg.rd("dv")
            xseg.rd("sc")
            for c in range(4):
                tns = p.op("dv", lambda e, c_=c: e.tensor_reduce(
                    nsum.t[:, c_:c_ + 1], xseg.t[:, D * c_:D * (c_ + 1)],
                    AX.X, ALU.add, negate=True), tDV)
                nsum.set_w(tDV, tns)
                sqscr.wrg("sc")
                tsq_ = p.op("sc", lambda e, c_=c: e.activation(
                    sqscr.t[:], xseg.t[:, D * c_:D * (c_ + 1)], AF.Square,
                    accum_out=sqsum.t[:, c_:c_ + 1]), tAC)
                sqscr.set_w(tAC, tsq_)
                sqsum.set_w(tAC, tsq_)
            xseg.add_rd(tAC, tAC.n)
            nsum.rd("dv")
            tnm = p.op("dv", lambda e: e.tensor_scalar(
                nsum.t[:], nsum.t[:], 1.0 / D, None, ALU.mult), tDV)
            nsum.set_w(tDV, tnm)
            tm2 = p.op("dv", lambda e: e.tensor_tensor(
                m2t.t[:], nsum.t[:], nsum.t[:], ALU.mult), tDV)
            m2t.set_w(tDV, tm2)
            p.wait("dv", tAC, tAC.n)
            tvv = p.op("dv", lambda e: e.scalar_tensor_tensor(
                varv.t[:], sqsum.t[:], 1.0 / D, m2t.t[:], ALU.mult, ALU.subtract), tDV)
            varv.set_w(tDV, tvv)
            sqsum.add_rd(tDV, tvv)
            m2t.add_rd(tDV, tvv)
            varv.rd("sc")
            trsd = p.op("sc", lambda e: e.activation(
                rstd.t[:], varv.t[:], AF.Sqrt, bias=epsT[:, 0:1]), tAC)
            rstd.set_w(tAC, trsd)
            varv.add_rd(tAC, trsd)
            rstd.rd("dv")
            trsd2 = p.op("dv", lambda e: e.reciprocal(rstd.t[:], rstd.t[:]), tDV)
            rstd.set_w(tDV, trsd2)
            rstd.rd("dv")
            nsum.rd("dv")
            x2b.wrg("sc")
            for c in range(4):
                tx2 = p.op("dv", lambda e, c_=c: e.tensor_scalar(
                    xseg.t[:, D * c_:D * (c_ + 1)], xseg.t[:, D * c_:D * (c_ + 1)],
                    nsum.t[:, c_:c_ + 1], rstd.t[:, c_:c_ + 1], ALU.add, ALU.mult), tDV)
                xseg.set_w(tDV, tx2)
                xseg.dspill("sc", lambda e, c_=c, s_=s:
                            e.dma_start(x2_s[SEG * s_ + 128 * c_:SEG * s_ + 128 * (c_ + 1), :],
                                        xseg.t[:, D * c_:D * (c_ + 1)]))
                tcv = p.op("sc", lambda e, c_=c: e.activation(
                    x2b.t[:, D * c_:D * (c_ + 1)], xseg.t[:, D * c_:D * (c_ + 1)],
                    AF.Copy), tAC)
                xseg.add_rd(tAC, tcv)
            x2b.set_w(tAC, tAC.n)
            nsum.add_rd(tDV, tDV.n)
            rstd.add_rd(tDV, tDV.n)
            x2b.rd("pe")
            for kc in range(KC):
                stg = x2Tst[kc % 2]
                stg.wrg("dv")
                blk, eo = divmod(kc, 2)
                for c in range(4):
                    bkt = banks[6 + (c % 2)]
                    bkt.guard_write("pe")
                    tt = p.op("pe", lambda e, b=bkt.bf(), c_=c, bl=blk, e_=eo:
                              e.transpose(b[:, 0:128],
                                          x2b.t[:, D * c_ + 256 * bl + e_:D * c_ + 256 * (bl + 1):2],
                                          idbT[:]), tPE)
                    bkt.set_w(tPE, tt)
                    bkt.guard_read("dv")
                    ev = p.op("dv", lambda e, b=bkt.bf(), c_=c, sg_=stg.t:
                              e.tensor_copy(sg_[:, 128 * c_:128 * (c_ + 1)], b[:, 0:128]), tDV)
                    bkt.add_reader(tDV, ev)
                stg.set_w(tDV, tDV.n)
                stg.dspill("sc", lambda e, kc_=kc, s_=s, d=x2T_s, st_=stg.t:
                           e.dma_start(d[128 * kc_:128 * (kc_ + 1), SEG * s_:SEG * (s_ + 1)],
                                       st_[:]))
            x2b.add_rd(tPE, tPE.n)
            if s + 1 < _DBG.get("nseg", NSEG):
                load_xseg(s + 1)
        sc2.close()
        barrier("p2end")

    if want("P4"):
        sc4 = ExitStack()

        def sb4(name, shape, dt):
            return Buf(p, sc4.enter_context(nc.sbuf_tensor(name, shape, dt)))

        x2te = sb4("x2te", [128, KC * 512], BF16)
        inter = sb4("inter", [128, 16 * 512], BF16)
        accb = sb4("accb", [128, 16 * 512], F32)
        wds = sb4("wds", [128, 8 * D], BF16)
        wgb = [sb4(f"wgb{i}", [128, KC * 256], BF16) for i in range(2)]
        wub = [sb4(f"wub{i}", [128, KC * 256], BF16) for i in range(2)]
        gq = sb4("gq", [128, 512], F32)
        gr = sb4("gr", [128, 512], F32)
        gt_ = gq
        gw_ = sb4("gww", [128, 512], F32)
        gA = gw_
        x2nat = [sb4("x2nat0", [128, D], F32)]
        natt = [sb4(f"natt{i}", [128, D], F32) for i in range(2)]
        nsum4 = sb4("nsum4", [128, 1], F32)
        sqsum4 = sb4("sqsum4", [128, 1], F32)
        m2t4 = sb4("m2t4", [128, 1], F32)
        varv4 = sb4("varv4", [128, 1], F32)
        rstd4 = sb4("rstd4", [128, 1], F32)
        sqscr4 = inter

        NTE = 8
        for te in range(NTE):
            x2te.dload("sy", lambda e, te_=te, d=x2te.t:
                       e.dma_start(d[:].rearrange("q (kc n) -> q kc n", kc=KC),
                                   x2T_s[:, 512 * te_:512 * (te_ + 1)]
                                   .rearrange("(kc q) n -> q kc n", q=128)))
            for sp_ in range(8):
                dq_pump(96)
                for sm_, v_ in dqs["group_tok"].get(sp_, []):
                    p.wait("sy", sm_, v_)
                for bl in range(8):
                    if te == 0:
                        dq_pump(2)
                    if bl == 0:
                        wds.dload("sy", lambda e, sp2=sp_, d=wds.t:
                                  e.dma_start(d[:].rearrange("q (kc n) -> q kc n", kc=8),
                                              wd_s[2048 * sp2:2048 * sp2 + 1024, :]
                                              .rearrange("(kc q) n -> q kc n", q=128)))
                    wg2 = wgb[bl % 2]
                    wu2 = wub[bl % 2]
                    wg2.dload("sy", lambda e, sp2=sp_, bl_=bl, d=wg2.t:
                              e.dma_start(d[:].rearrange("q (kc n) -> q kc n", kc=KC),
                                          wg_s[:, 2048 * sp2 + 256 * bl_:2048 * sp2 + 256 * (bl_ + 1)]
                                          .rearrange("(kc q) n -> q kc n", q=128)))
                    wu2.dload("sy", lambda e, sp2=sp_, bl_=bl, d=wu2.t:
                              e.dma_start(d[:].rearrange("q (kc n) -> q kc n", kc=KC),
                                          wu_s[:, 2048 * sp2 + 256 * bl_:2048 * sp2 + 256 * (bl_ + 1)]
                                          .rearrange("(kc q) n -> q kc n", q=128)))
                    for eo in range(2):
                        jl = 2 * bl + eo
                        bkg = banks[jl % 2]
                        bku = banks[2 + (jl % 2)]
                        bkg.guard_write("pe")
                        bku.guard_write("pe")
                        wg2.rd("pe")
                        wu2.rd("pe")
                        x2te.rd("pe")
                        for kc in range(KC):
                            p.op("pe", lambda e, b=bkg.ap(), kc_=kc, e_=eo, w_=wg2.t:
                                 e.matmul(b, w_[:, 256 * kc_ + e_:256 * (kc_ + 1):2],
                                          x2te.t[:, 512 * kc_:512 * (kc_ + 1)],
                                          start=(kc_ == 0), stop=(kc_ == KC - 1)),
                                 tick=tPE if kc == KC - 1 else None)
                        bkg.set_w(tPE, tPE.n)
                        for kc in range(KC):
                            p.op("pe", lambda e, b=bku.ap(), kc_=kc, e_=eo, w_=wu2.t:
                                 e.matmul(b, w_[:, 256 * kc_ + e_:256 * (kc_ + 1):2],
                                          x2te.t[:, 512 * kc_:512 * (kc_ + 1)],
                                          start=(kc_ == 0), stop=(kc_ == KC - 1)),
                                 tick=tPE if kc == KC - 1 else None)
                        bku.set_w(tPE, tPE.n)
                        wg2.add_rd(tPE, tPE.n)
                        wu2.add_rd(tPE, tPE.n)
                        bkg.guard_read("sc")
                        gq.wrg("sc")
                        t_q = p.op("sc", lambda e, b=bkg.ap(): e.activation(
                            gq.t[:], b, AF.Square, scale=0.5), tAC)
                        gq.set_w(tAC, t_q)
                        gq.rd("dv")
                        gr.wrg("dv")
                        t_r = p.op("dv", lambda e: e.tensor_scalar(
                            gr.t[:], gq.t[:], 0.044715, 1.0, ALU.mult, ALU.add), tDV)
                        gr.set_w(tDV, t_r)
                        gq.add_rd(tDV, t_r)
                        bkg.guard_read("dv")
                        gr.rd("dv")
                        gt_.wrg("dv")
                        t_t = p.op("dv", lambda e, b=bkg.ap(): e.scalar_tensor_tensor(
                            gt_.t[:], b, 0.5 * 0.7978845608028654, gr.t[:],
                            ALU.mult, ALU.mult), tDV)
                        gt_.set_w(tDV, t_t)
                        gr.add_rd(tDV, t_t)
                        gt_.rd("sc")
                        gw_.wrg("sc")
                        t_w = p.op("sc", lambda e: e.activation(
                            gw_.t[:], gt_.t[:], AF.Tanh), tAC)
                        gw_.set_w(tAC, t_w)
                        gt_.add_rd(tAC, t_w)
                        gw_.rd("dv")
                        gA.wrg("dv")
                        t_A = p.op("dv", lambda e, b=bkg.ap(): e.scalar_tensor_tensor(
                            gA.t[:], gw_.t[:], 1.0, b, ALU.add, ALU.mult), tDV)
                        gA.set_w(tDV, t_A)
                        gw_.add_rd(tDV, t_A)
                        bkg.add_reader(tDV, t_A)
                        bku.guard_read("dv")
                        gA.rd("dv")
                        t_i = p.op("dv", lambda e, b=bku.ap(), jl_=jl:
                                   e.scalar_tensor_tensor(
                                       inter.t[:, 512 * jl_:512 * (jl_ + 1)],
                                       gA.t[:], 0.125, b, ALU.mult, ALU.mult), tDV)
                        gA.add_rd(tDV, t_i)
                        bku.add_reader(tDV, t_i)
                        inter.set_w(tDV, tDV.n)
                inter.rd("pe")
                for hf in range(2):
                    if hf == 1:
                        wds.dload("sy", lambda e, sp2=sp_, d=wds.t:
                                  e.dma_start(d[:].rearrange("q (kc n) -> q kc n", kc=8),
                                              wd_s[2048 * sp2 + 1024:2048 * sp2 + 2048, :]
                                              .rearrange("(kc q) n -> q kc n", q=128)))
                    wds.rd("pe")
                    for hc in range(16):
                        bkd = banks[4 + (hc % 2)]
                        bkd.guard_write("pe")
                        for jl in range(8):
                            jg = 8 * hf + jl
                            p.op("pe", lambda e, b=bkd.ap(), jl_=jl, jg_=jg, hc_=hc:
                                 e.matmul(b, wds.t[:, D * jl_ + 128 * hc_:D * jl_ + 128 * (hc_ + 1)],
                                          inter.t[:, 512 * jg_:512 * (jg_ + 1)],
                                          start=(jl_ == 0), stop=(jl_ == 7)),
                                 tick=tPE if jl == 7 else None)
                        bkd.set_w(tPE, tPE.n)
                        if sp_ == 0 and hf == 0:
                            bkd.guard_read("sc")
                            accb.wrg("sc")
                            tacc = p.op("sc", lambda e, b=bkd.ap(), hc_=hc:
                                        e.activation(accb.t[:, 512 * hc_:512 * (hc_ + 1)],
                                                     b, AF.Copy), tAC)
                            accb.set_w(tAC, tacc)
                            bkd.add_reader(tAC, tacc)
                        else:
                            bkd.guard_read("dv")
                            accb.rd("dv")
                            tacc = p.op("dv", lambda e, b=bkd.ap(), hc_=hc:
                                        e.tensor_tensor(
                                            accb.t[:, 512 * hc_:512 * (hc_ + 1)],
                                            accb.t[:, 512 * hc_:512 * (hc_ + 1)],
                                            b, ALU.add), tDV)
                            accb.set_w(tDV, tacc)
                            bkd.add_reader(tDV, tacc)
                    wds.add_rd(tPE, tPE.n)
                inter.add_rd(tPE, tPE.n)
            accb.rd("pe")
            for tc4 in range(4):
                xn = x2nat[0]
                nt = natt[tc4 % 2]
                xn.dload("sy", lambda e, te_=te, tc_=tc4, d=xn.t:
                         e.dma_start(d[:], x2_s[512 * te_ + 128 * tc_:512 * te_ + 128 * (tc_ + 1), :]))
                nt.wrg("dv")
                xn.rd("dv")
                for hc in range(16):
                    bkt = banks[6 + (hc % 2)]
                    bkt.guard_write("pe")
                    tt = p.op("pe", lambda e, b=bkt.ap(), hc_=hc, tc_=tc4:
                              e.transpose(b[:, 0:128],
                                          accb.t[:, 512 * hc_ + 128 * tc_:512 * hc_ + 128 * (tc_ + 1)],
                                          idfT[:]), tPE)
                    bkt.set_w(tPE, tt)
                    bkt.guard_read("dv")
                    ev = p.op("dv", lambda e, b=bkt.ap(), hc_=hc, n_=nt.t, x_=xn.t:
                              e.scalar_tensor_tensor(
                                  n_[:, 128 * hc_:128 * (hc_ + 1)],
                                  b[:, 0:128], 0.5,
                                  x_[:, 128 * hc_:128 * (hc_ + 1)],
                                  ALU.mult, ALU.add), tDV)
                    bkt.add_reader(tDV, ev)
                nt.set_w(tDV, tDV.n)
                xn.add_rd(tDV, tDV.n)
                nt.rd("dv")
                nt.rd("sc")
                tns = p.op("dv", lambda e, n_=nt.t: e.tensor_reduce(
                    nsum4.t[:], n_[:], AX.X, ALU.add, negate=True), tDV)
                nsum4.set_w(tDV, tns)
                sqscr4.wrg("sc")
                tsq_ = p.op("sc", lambda e, n_=nt.t: e.activation(
                    sqscr4.t[:, 0:D].bitcast(BF16)[:, 0:D], n_[:], AF.Square,
                    accum_out=sqsum4.t[:]), tAC)
                sqscr4.set_w(tAC, tsq_)
                sqsum4.set_w(tAC, tsq_)
                nt.add_rd(tAC, tsq_)
                nsum4.rd("dv")
                tnm = p.op("dv", lambda e: e.tensor_scalar(
                    nsum4.t[:], nsum4.t[:], 1.0 / D, None, ALU.mult), tDV)
                nsum4.set_w(tDV, tnm)
                tm2 = p.op("dv", lambda e: e.tensor_tensor(
                    m2t4.t[:], nsum4.t[:], nsum4.t[:], ALU.mult), tDV)
                m2t4.set_w(tDV, tm2)
                p.wait("dv", tAC, tsq_)
                tvv = p.op("dv", lambda e: e.scalar_tensor_tensor(
                    varv4.t[:], sqsum4.t[:], 1.0 / D, m2t4.t[:], ALU.mult, ALU.subtract), tDV)
                varv4.set_w(tDV, tvv)
                sqsum4.add_rd(tDV, tvv)
                m2t4.add_rd(tDV, tvv)
                varv4.rd("sc")
                trsd = p.op("sc", lambda e: e.activation(
                    rstd4.t[:], varv4.t[:], AF.Sqrt, bias=epsT[:, 0:1]), tAC)
                rstd4.set_w(tAC, trsd)
                varv4.add_rd(tAC, trsd)
                rstd4.rd("dv")
                trsd2 = p.op("dv", lambda e: e.reciprocal(rstd4.t[:], rstd4.t[:]), tDV)
                rstd4.set_w(tDV, trsd2)
                rstd4.rd("dv")
                nsum4.rd("dv")
                tno = p.op("dv", lambda e, n_=nt.t: e.tensor_scalar(
                    n_[:], n_[:], nsum4.t[:], rstd4.t[:], ALU.add, ALU.mult), tDV)
                nt.set_w(tDV, tno)
                nsum4.add_rd(tDV, tno)
                rstd4.add_rd(tDV, tno)
                nt.dspill("po", lambda e, te_=te, tc_=tc4, n_=nt.t:
                          e.dma_start(out_d[512 * te_ + 128 * tc_:512 * te_ + 128 * (tc_ + 1), :],
                                      n_[:]))
            accb.add_rd(tPE, tPE.n)
        sc4.close()
    else:
        dq_pump(10 ** 6)
        zf0 = sbuf("zf0", [128, D], F32)
        t0_ = p.op("dv", lambda e: e.memset(zf0[:], 0.0), tDV)
        p.wait("po", tDV, t0_)
        for tc in range(32):
            p.dma("po", lambda e, tc_=tc: e.dma_start(
                out_d[128 * tc_:128 * (tc_ + 1), :], zf0[:]), qOut)

    for sm_ in p.all_sems:
        if sm_.n > 0 and sm_.name[0] in "bq":
            p.wait("po", sm_, sm_.n)
    p.wait("po", qOut, qOut.n)

    with nc.Block() as block:
        @block.sync
        def _(eng):
            for f in p.rec["sy"]:
                f(eng)

        @block.scalar
        def _(eng):
            for f in p.rec["sc"]:
                f(eng)

        @block.vector
        def _(eng):
            for f in p.rec["dv"]:
                f(eng)

        @block.tensor
        def _(eng):
            for f in p.rec["pe"]:
                f(eng)

        @block.gpsimd
        def _(eng):
            for f in p.rec["po"]:
                f(eng)

    es.close()
    return nc


def kernel(**inputs):
    maps = _host_prep(inputs)
    nc = build_program()
    res = run_bass_kernel_spmd(nc, maps, core_ids=list(range(8)))
    return np.stack([np.asarray(r["out"]) for r in res.results], axis=0).astype(np.float32)


def kernel(**inputs):
    maps = _host_prep(inputs)
    nc = build_program()
    res = run_bass_kernel_spmd(nc, maps, core_ids=list(range(8)))
    return np.stack([np.asarray(r["out"]) for r in res.results], axis=0).astype(np.float32)


# version 07
# speedup vs baseline: 1.0848x; 1.0848x over previous
import math
from contextlib import ExitStack

import numpy as np
import ml_dtypes

import concourse.bass as bass
import concourse.mybir as mybir
from concourse.bass_utils import run_bass_kernel_spmd

F32 = mybir.dt.float32
BF16 = mybir.dt.bfloat16
I32 = mybir.dt.int32
U16 = mybir.dt.uint16
AF = mybir.ActivationFunctionType
ALU = mybir.AluOpType
AX = mybir.AxisListType

S = 4096
D = 2048
H = 16
SEG = 512
NSEG = S // SEG
KC = D // 128
MLP = 16384
NEG = -1e30
EPS = 1e-5
ISQ = 1.0 / math.sqrt(128.0)
MEMW = 136

_G = [0, 1, 2, 3, 4, 6, 8, 12]
TAB32 = 0
for _j, _g in enumerate(_G):
    TAB32 |= _g << (4 * _j)
TAB32 -= 1 << 32

_DBG = {}


def _perm(n):
    q = np.empty(n, dtype=np.int64)
    for b in range(n // 256):
        base = 256 * b
        q[base:base + 128] = base + np.arange(0, 256, 2)
        q[base + 128:base + 256] = base + np.arange(1, 256, 2)
    return q


PERM_D = _perm(D)


def _host_prep(inputs):
    x = np.ascontiguousarray(np.asarray(inputs["L0_attn_x"], dtype=np.float32))
    wq = np.ascontiguousarray(np.asarray(inputs["L0_attn_Wq"], dtype=np.float32)[PERM_D, :])
    wk = np.ascontiguousarray(np.asarray(inputs["L0_attn_Wk"], dtype=np.float32)[PERM_D, :])
    wv = np.ascontiguousarray(np.asarray(inputs["L0_attn_Wv"], dtype=np.float32)[PERM_D, :])
    wo = np.ascontiguousarray(np.asarray(inputs["L0_attn_Wo"], dtype=np.float32))
    beta = np.ascontiguousarray(np.asarray(inputs["L0_attn_betas"], dtype=np.float32).reshape(H, 128))
    gw = np.ascontiguousarray(np.asarray(inputs["L0_ffn_gate_W_packed"], dtype=np.int32).T)
    uw = np.ascontiguousarray(np.asarray(inputs["L0_ffn_up_W_packed"], dtype=np.int32).T)
    dw = np.ascontiguousarray(np.asarray(inputs["L0_ffn_down_W_packed"], dtype=np.int32).T)
    gsc = np.ascontiguousarray(np.asarray(inputs["L0_ffn_gate_scales"], dtype=np.float32).T)
    usc = np.ascontiguousarray(np.asarray(inputs["L0_ffn_up_scales"], dtype=np.float32).T)
    dsc = np.ascontiguousarray(np.asarray(inputs["L0_ffn_down_scales"], dtype=np.float32).T)
    mask = np.where(np.arange(128)[None, :] <= np.arange(128)[:, None],
                    np.float32(0.0), np.float32(NEG)).astype(np.float32)
    idf = np.eye(128, dtype=np.float32)
    idb = np.eye(128, dtype=ml_dtypes.bfloat16)
    common = dict(wq=wq, wk=wk, wv=wv, wo=wo, beta=beta, gw=gw, uw=uw, dw=dw,
                  gsc=gsc, usc=usc, dsc=dsc, mask=mask, idf=idf, idb=idb)
    return [dict(common, x=np.ascontiguousarray(x[b])) for b in range(8)]


class Sem:
    __slots__ = ("h", "n", "name")

    def __init__(self, h, name):
        self.h = h
        self.n = 0
        self.name = name


class Plan:
    ENGS = ("sy", "sc", "dv", "pe", "po")

    def __init__(self, nc, es):
        self.nc = nc
        self.es = es
        self.rec = {k: [] for k in self.ENGS}
        self.wm = {}
        self.all_sems = []
        self.ticks = {}

    def sem(self, name):
        s = Sem(self.es.enter_context(self.nc.semaphore(name)), name)
        self.all_sems.append(s)
        return s

    def wait(self, ek, sem, val):
        if val is None or val <= 0:
            return
        key = (ek, sem.name)
        if self.wm.get(key, 0) >= val:
            return
        self.wm[key] = val
        self.rec[ek].append(lambda eng, h=sem.h, v=val: eng.wait_ge(h, v))

    def op(self, ek, f, tick=None, k=1):
        own = self.ticks.get(ek)
        if own is not None and own.n > 0:
            self.wait(ek, own, own.n)
        if tick is None:
            self.rec[ek].append(lambda eng, f=f: f(eng))
            return None
        tick.n += k

        def g(eng, f=f, h=tick.h, k=k):
            f(eng).then_inc(h, k)
        self.rec[ek].append(g)
        return tick.n

    def dma(self, ek, f, qsem):
        return self.op(ek, f, tick=qsem, k=16)


class Bank:

    def __init__(self, plan, handle, idx):
        self.p = plan
        self.h = handle
        self.idx = idx
        self.w = None
        self.readers = {}

    def ap(self):
        return self.h[:]

    def bf(self):
        return self.h[:].bitcast(BF16)

    def guard_write(self, ek="pe"):
        for s, v in self.readers.items():
            self.p.wait(ek, s, v)
        self.readers = {}

    def set_w(self, sem, val):
        self.w = (sem, val)

    def guard_read(self, ek):
        if self.w:
            self.p.wait(ek, self.w[0], self.w[1])

    def add_reader(self, sem, val):
        if val is not None:
            self.readers[sem] = max(self.readers.get(sem, 0), val)


class Buf:

    _n = [0]

    def __init__(self, plan, ap):
        self.p = plan
        self.t = ap
        self.w = None
        self.readers = {}
        self._qsem = None
        self._ssem = None

    def qsem(self):
        if self._qsem is None:
            Buf._n[0] += 1
            self._qsem = self.p.sem(f"bq{Buf._n[0]}")
        return self._qsem

    def ssem(self):
        if self._ssem is None:
            Buf._n[0] += 1
            self._ssem = self.p.sem(f"bs{Buf._n[0]}")
        return self._ssem

    def dload(self, ek, fn):
        self.wrg(ek)
        q = self.qsem()
        self.p.dma(ek, fn, q)
        self.set_w(q, q.n)
        return q.n

    def dspill(self, ek, fn):
        self.rd(ek)
        q = self.ssem()
        self.p.dma(ek, fn, q)
        self.add_rd(q, q.n)
        return q.n

    def rd(self, ek):
        if self.w:
            self.p.wait(ek, self.w[0], self.w[1])

    def add_rd(self, sem, val):
        if val is not None:
            self.readers[sem] = max(self.readers.get(sem, 0), val)

    def wrg(self, ek):
        for s, v in self.readers.items():
            self.p.wait(ek, s, v)
        self.readers = {}
        if self.w:
            self.p.wait(ek, self.w[0], self.w[1])

    def set_w(self, sem, val):
        self.w = (sem, val)


def build_program():
    taps = _DBG.get("taps", set())
    stop = _DBG.get("stop", "P4")
    ph_order = ["P1", "P2", "P4"]

    def want(ph):
        return ph_order.index(ph) <= ph_order.index(stop)

    nc = bass.Bass("TRN2", target_bir_lowering=False)
    es = ExitStack()
    p = Plan(nc, es)

    def dram(name, shape, dt, kind):
        if kind == "Internal" and name in taps:
            kind = "ExternalOutput"
        return nc.dram_tensor(name, shape, dt, kind=kind).ap()

    x_in = dram("x", [S, D], F32, "ExternalInput")
    w_in = {"q": dram("wq", [D, D], F32, "ExternalInput"),
            "k": dram("wk", [D, D], F32, "ExternalInput"),
            "v": dram("wv", [D, D], F32, "ExternalInput")}
    wo_in = dram("wo", [D, D], F32, "ExternalInput")
    beta_in = dram("beta", [H, 128], F32, "ExternalInput")
    gw_in = dram("gw", [1024, MLP], I32, "ExternalInput")
    uw_in = dram("uw", [1024, MLP], I32, "ExternalInput")
    dw_in = dram("dw", [MLP // 2, D], I32, "ExternalInput")
    gsc_in = dram("gsc", [64, MLP], F32, "ExternalInput")
    usc_in = dram("usc", [64, MLP], F32, "ExternalInput")
    dsc_in = dram("dsc", [512, D], F32, "ExternalInput")
    mask_in = dram("mask", [128, 128], F32, "ExternalInput")
    idf_in = dram("idf", [128, 128], F32, "ExternalInput")
    idb_in = dram("idb", [128, 128], BF16, "ExternalInput")
    out_d = dram("out", [S, D], F32, "ExternalOutput")

    qT_s = dram("qT_s", [D, S], BF16, "Internal")
    kT_s = dram("kT_s", [D, S], BF16, "Internal")
    v_s = dram("v_s", [S, D], BF16, "Internal")
    wo_s2 = dram("wo_s2", [16 * D, 128], BF16, "Internal")
    x2_s = dram("x2_s", [S, D], F32, "Internal")
    x2T_s = dram("x2T_s", [D, S], BF16, "Internal")
    wg_s = dram("wg_s", [D, MLP], BF16, "Internal")
    wu_s = dram("wu_s", [D, MLP], BF16, "Internal")
    wd_s = dram("wd_s", [MLP, D], BF16, "Internal")

    tPE = p.sem("tPE")
    tDV = p.sem("tDV")
    tAC = p.sem("tAC")
    tPO = p.sem("tPO")
    p.ticks = {"dv": tDV, "sc": tAC, "po": tPO}
    qCst = p.sem("qCst")
    qW = p.sem("qW")
    qX = p.sem("qX")
    qSeg = p.sem("qSeg")
    qPk = p.sem("qPk")
    qF = p.sem("qF")
    qSpA = p.sem("qSpA")
    qSpX = p.sem("qSpX")
    qSpW = p.sem("qSpW")
    qOut = p.sem("qOut")

    def sbuf(name, shape, dt):
        return es.enter_context(nc.sbuf_tensor(name, shape, dt))

    maskT = sbuf("maskT", [128, 128], F32)
    idfT = sbuf("idfT", [128, 128], F32)
    idbT = sbuf("idbT", [128, 128], BF16)
    betaT = sbuf("betaT", [H, 128], F32)
    btmp = sbuf("btmp", [H, 128], F32)
    bT = sbuf("bT", [128, H], F32)
    bTo = sbuf("bTo", [128, H], F32)
    onesb = sbuf("onesb", [128, 128], BF16)
    memf = sbuf("memf", [128, H * MEMW], F32)
    memb = sbuf("memb", [128, H * MEMW], BF16)
    zf = sbuf("zf", [128, H], F32)
    zrepb = sbuf("zrepb", [128, 128], BF16)
    epsT = sbuf("epsT", [128, 1], F32)
    negoneT = sbuf("negoneT", [128, 1], F32)
    nc.const_aps.register(F32, -1.0, negoneT[:])

    DQN = 512
    dq_in = [Buf(p, sbuf(f"dq_in{i}", [128, DQN], I32)) for i in range(2)]
    dq_sc = [Buf(p, sbuf(f"dq_sc{i}", [128, DQN], F32)) for i in range(2)]
    dq_t = {nm: Buf(p, sbuf(f"dq_{nm}", [128, DQN], I32))
            for nm in ("aL", "aH", "gL", "gH", "bL", "bH")}
    dq_gfL = Buf(p, sbuf("dq_gfL", [128, DQN], F32))
    dq_gfH = Buf(p, sbuf("dq_gfH", [128, DQN], F32))
    dq_m1L = Buf(p, sbuf("dq_m1L", [128, DQN], F32))
    dq_m1H = Buf(p, sbuf("dq_m1H", [128, DQN], F32))
    dq_tab = sbuf("dq_tab", [128, 1], I32)
    dq_wo = [Buf(p, sbuf(f"dq_wo{i}", [128, DQN], BF16)) for i in range(4)]

    banks = [Bank(p, es.enter_context(nc.psum_tensor(f"psb{i}", [128, 512], F32)), i)
             for i in range(8)]

    dq_specs = [(gw_in, gsc_in, wg_s, 8), (uw_in, usc_in, wu_s, 8),
                (dw_in, dsc_in, wd_s, 64)]
    dq_tiles = []
    for sgrp in range(8):
        for wi in (0, 1):
            for n4 in range(4):
                for kk in range(8):
                    dq_tiles.append((wi, kk, 2048 * sgrp + 512 * n4))
        for kk in range(8 * sgrp, 8 * sgrp + 8):
            for n4 in range(4):
                dq_tiles.append((2, kk, 512 * n4))
    NDQ = len(dq_tiles)
    dqs = {"dma": 0, "pool": 0, "dve": 0, "pool_t": [], "dve_t": [],
           "spill_tok": [], "group_tok": {}}

    def dq_issue_dma():
        i = dqs["dma"]
        if i >= NDQ:
            return True
        wi, kk, n0 = dq_tiles[i]
        pin, sin_, _, _ = dq_specs[wi]
        bi = i % 2
        dq_in[bi].dload("sy", lambda e, d=dq_in[bi].t,
                        s=pin[128 * kk:128 * (kk + 1), n0:n0 + DQN]:
                        e.dma_start(d[:], s))
        dq_sc[bi].dload("sy", lambda e, d=dq_sc[bi].t,
                        s=sin_[8 * kk:8 * (kk + 1), n0:n0 + DQN]
                        .unsqueeze(1).broadcast_to([8, 16, DQN]):
                        e.dma_start(d[:], s))
        dqs["dma"] = i + 1
        return True

    NSUBT = 1

    def dq_stageA(ti):
        while dqs["dma"] < ti + 1:
            dq_issue_dma()
        bi = ti % 2
        src_ = dq_in[bi].t[:]
        tb = dq_tab[:, 0:1].broadcast_to([128, DQN])
        dq_in[bi].rd("dv")
        for nm, f in (
            ("aL", lambda e, o, s: e.tensor_scalar(o[:], s, 2, 28, ALU.logical_shift_left, ALU.bitwise_and)),
            ("aH", lambda e, o, s: e.tensor_scalar(o[:], s, 2, 28, ALU.logical_shift_right, ALU.bitwise_and)),
            ("bL", lambda e, o, s: e.tensor_scalar(o[:], s, 8, None, ALU.bitwise_and)),
            ("bH", lambda e, o, s: e.tensor_scalar(o[:], s, 128, None, ALU.bitwise_and)),
        ):
            buf = dq_t[nm]
            buf.wrg("dv")
            p.op("dv", lambda e, o=buf.t, s=src_, f_=f: f_(e, o, s), tDV)
            buf.set_w(tDV, tDV.n)
        for gnm, anm in (("gL", "aL"), ("gH", "aH")):
            g, a = dq_t[gnm], dq_t[anm]
            g.wrg("dv")
            a.rd("dv")
            p.op("dv", lambda e, o=g.t, ai=a.t, t=tb: e.tensor_tensor(
                o[:], t, ai[:], ALU.logical_shift_right), tDV)
            g.set_w(tDV, tDV.n)
            a.add_rd(tDV, tDV.n)
            g.rd("dv")
            tgm = p.op("dv", lambda e, o=g.t: e.tensor_scalar(
                o[:], o[:], 15, None, ALU.bitwise_and), tDV)
            g.set_w(tDV, tgm)
        dq_in[bi].add_rd(tDV, tDV.n)
        for gnm, bnm, gf, m1, scl in (("gL", "bL", dq_gfL, dq_m1L, -0.25),
                                      ("gH", "bH", dq_gfH, dq_m1H, -0.015625)):
            g, b = dq_t[gnm], dq_t[bnm]
            g.rd("sc")
            gf.wrg("sc")
            t1 = p.op("sc", lambda e, o=gf.t, i=g.t: e.activation(o[:], i[:], AF.Copy), tAC)
            gf.set_w(tAC, t1)
            g.add_rd(tAC, t1)
            b.rd("sc")
            m1.wrg("sc")
            t2 = p.op("sc", lambda e, o=m1.t, i=b.t, s=scl: e.activation(
                o[:], i[:], AF.Copy, scale=s, bias=1.0), tAC)
            m1.set_w(tAC, t2)
            b.add_rd(tAC, t2)
        wi, kk, n0 = dq_tiles[ti]
        woL = dq_wo[2 * (ti % 2)]
        woH = dq_wo[2 * (ti % 2) + 1]
        dq_sc[bi].rd("po")
        for gf, m1, wo_ in ((dq_gfL, dq_m1L, woL), (dq_gfH, dq_m1H, woH)):
            gf.rd("po")
            p.op("po", lambda e, o=gf.t, b=dq_sc[bi].t: e.tensor_tensor(
                o[:], o[:], b[:], ALU.mult), tPO)
            gf.set_w(tPO, tPO.n)
            m1.rd("po")
            wo_.wrg("po")
            tw = p.op("po", lambda e, o=wo_.t, a=gf.t, b=m1.t: e.tensor_tensor(
                o[:], a[:], b[:], ALU.mult), tPO)
            wo_.set_w(tPO, tw)
            gf.add_rd(tPO, tw)
            m1.add_rd(tPO, tw)
        dq_sc[bi].add_rd(tPO, tPO.n)
        sp = dq_specs[wi][2]
        r0 = 256 * kk
        woL.dspill("sc", lambda e, d=sp[r0:r0 + 128, n0:n0 + DQN], s=woL.t:
                   e.dma_start(d, s[:]))
        woH.dspill("sc", lambda e, d=sp[r0 + 128:r0 + 256, n0:n0 + DQN], s=woH.t:
                   e.dma_start(d, s[:]))
        dqs["spill_tok"] = [(w._ssem, w._ssem.n) for w in dq_wo if w._ssem]

    def dq_dve_sub():
        si = dqs["dve"]
        if si >= NDQ:
            return False
        dq_stageA(si)
        dqs["dve"] = si + 1
        if (si + 1) % 96 == 0:
            dqs["group_tok"][(si + 1) // 96 - 1] = list(dqs["spill_tok"])
        return True

    def dq_pump(n):
        if _DBG.get("no_dq"):
            return
        for _ in range(n):
            if not dq_dve_sub():
                return

    def dq_group_spw(sgrp):
        return 96 * (sgrp + 1) * 2 * 16

    p.dma("sy", lambda e: e.dma_start(maskT[:], mask_in[:]), qCst)
    p.dma("sy", lambda e: e.dma_start(idfT[:], idf_in[:]), qCst)
    p.dma("sy", lambda e: e.dma_start(idbT[:], idb_in[:]), qCst)
    p.dma("sy", lambda e: e.dma_start(betaT[:], beta_in[:]), qCst)
    cst_n = qCst.n

    p.wait("dv", qCst, cst_n)
    p.op("dv", lambda e: e.memset(dq_tab[:], TAB32), tDV)
    p.op("dv", lambda e: e.memset(epsT[:], EPS), tDV)
    p.op("dv", lambda e: e.memset(negoneT[:], -1.0), tDV)
    p.op("dv", lambda e: e.memset(onesb[:], 1.0), tDV)
    p.op("dv", lambda e: e.memset(memf[:], 0.0), tDV)
    p.op("dv", lambda e: e.memset(memb[:], 0.0), tDV)
    p.op("dv", lambda e: e.memset(zf[:], 1.0 / 128.0), tDV)
    p.wait("dv", tDV, tDV.n)
    for h_ in range(H):
        t_z0 = p.op("dv", lambda e, c=MEMW * h_ + 128: e.memset(
            memb[:, c:c + 1], 1.0 / 128.0), tDV)
    t_dv_init = t_z0

    p.wait("sc", qCst, cst_n)
    t1 = p.op("sc", lambda e: e.activation(btmp[:], betaT[:], AF.Exp, scale=-1.0), tAC)
    p.wait("dv", tAC, t1)
    p.op("dv", lambda e: e.tensor_scalar(btmp[:], btmp[:], 1.0, None, ALU.add), tDV)
    p.wait("dv", tDV, tDV.n)
    t2 = p.op("dv", lambda e: e.reciprocal(btmp[:], btmp[:]), tDV)
    p.wait("pe", tDV, t2)
    p.wait("pe", qCst, cst_n)
    banks[0].guard_write("pe")
    t3 = p.op("pe", lambda e: e.transpose(banks[0].h[:, 0:H], btmp[:], idfT[0:H, 0:H]), tPE)
    banks[0].set_w(tPE, t3)
    banks[0].guard_read("dv")
    p.op("dv", lambda e: e.tensor_copy(bT[:], banks[0].h[:, 0:H]), tDV)
    p.wait("dv", tDV, tDV.n)
    t4 = p.op("dv", lambda e: e.tensor_scalar(bTo[:], bT[:], -1.0, 1.0, ALU.mult, ALU.add), tDV)
    banks[0].add_reader(tDV, t4)

    sc0 = ExitStack()

    def sb0(name, shape, dt):
        return sc0.enter_context(nc.sbuf_tensor(name, shape, dt))

    wstage = [Buf(p, sb0(f"wstage{i}", [128, 1024], F32)) for i in range(2)]
    xT = sb0("xT", [128, KC * S], BF16)
    xTb = Buf(p, xT)
    sc0b = ExitStack()
    xbst = [Buf(p, sc0b.enter_context(nc.sbuf_tensor(f"xbst{i}", [128, D], BF16)))
            for i in range(2)]
    xstage = wstage
    wbst = xbst

    for c in range(16):
        bo = c % 2
        for hlf in range(2):
            bi = hlf % 2
            wstage[bi].dload("sy", lambda e, d=wstage[bi].t,
                             s=wo_in[128 * c:128 * (c + 1), 1024 * hlf:1024 * (hlf + 1)]:
                             e.dma_start(d[:], s))
            wstage[bi].rd("sc")
            if hlf == 0:
                wbst[bo].wrg("sc")
            tt = p.op("sc", lambda e, o=wbst[bo].t[:, 1024 * hlf:1024 * (hlf + 1)],
                      i=wstage[bi].t: e.activation(o, i[:], AF.Copy), tAC)
            wstage[bi].add_rd(tAC, tt)
        wbst[bo].set_w(tAC, tt)
        dst = wo_s2.rearrange("(mc jv) m -> jv mc m", mc=16)[128 * c:128 * (c + 1)]
        wbst[bo].dspill("sc", lambda e, d=dst, s=wbst[bo].t:
                        e.dma_start(d, s[:].rearrange("q (mc m) -> q mc m", mc=16)))

    for tc in range(32):
        bo = tc % 2
        for hlf in range(2):
            bi = hlf % 2
            xstage[bi].dload("sy", lambda e, d=xstage[bi].t,
                             s=x_in[128 * tc:128 * (tc + 1), 1024 * hlf:1024 * (hlf + 1)]:
                             e.dma_start(d[:], s))
            xstage[bi].rd("sc")
            if hlf == 0:
                xbst[bo].wrg("sc")
            tt = p.op("sc", lambda e, o=xbst[bo].t[:, 1024 * hlf:1024 * (hlf + 1)],
                      i=xstage[bi].t: e.activation(o, i[:], AF.Copy), tAC)
            xstage[bi].add_rd(tAC, tt)
        xbst[bo].set_w(tAC, tt)
        xbst[bo].rd("pe")
        p.wait("pe", qCst, cst_n)
        last_ev = None
        for kc in range(KC):
            blk, eo = divmod(kc, 2)
            src = xbst[bo].t[:, 256 * blk + eo: 256 * (blk + 1): 2]
            bk = banks[kc % 2]
            bk.guard_write("pe")
            ttr = p.op("pe", lambda e, b=bk.bf(), s=src: e.transpose(
                b[:, 0:128], s, idbT[:]), tPE)
            bk.set_w(tPE, ttr)
            bk.guard_read("dv")
            ev = p.op("dv", lambda e, b=bk.bf(),
                      d=xT[:, kc * S + 128 * tc: kc * S + 128 * (tc + 1)]:
                      e.tensor_copy(d, b[:, 0:128]), tDV)
            bk.add_reader(tDV, ev)
            last_ev = ev
        xbst[bo].add_rd(tPE, tPE.n)
        dq_pump(1)
    xTb.set_w(tDV, tDV.n)

    wbst_for_gate = list(wbst)
    for ek in ("sy", "sc", "dv", "pe"):
        for sem in (tPE, tDV, tAC):
            p.wait(ek, sem, sem.n)
        for b_ in list(wstage) + list(xbst):
            for sm_ in (b_._qsem, b_._ssem):
                if sm_ is not None and sm_.n > 0:
                    p.wait(ek, sm_, sm_.n)
    sc0b.close()
    pjw = Buf(p, sb0("pjw", [128, KC * 512], BF16))
    pjo = Buf(p, sb0("pjo", [128, S // 2], BF16))
    vout = [Buf(p, sb0(f"vout{i}", [128, 512], BF16)) for i in range(2)]

    def load_wquarter(win, jq):
        last = None
        for kc in range(KC):
            bi = kc % 2
            wstage[bi].dload("sy", lambda e, d=wstage[bi].t[:, 0:512],
                             s=win[128 * kc:128 * (kc + 1), 512 * jq:512 * (jq + 1)]:
                             e.dma_start(d, s))
            wstage[bi].rd("sc")
            last = p.op("sc", lambda e, o=pjw.t[:, 512 * kc:512 * (kc + 1)],
                        i=wstage[bi].t[:, 0:512]: e.activation(o, i, AF.Copy), tAC)
            wstage[bi].add_rd(tAC, last)
        pjw.set_w(tAC, last)

    if want("P1"):
        xTb.rd("pe")
        for pj, dst_T in (("q", qT_s), ("k", kT_s)):
            for jq in range(_DBG.get("p1_jq", 4)):
                pjw.wrg("sc")
                load_wquarter(w_in[pj], jq)
                pjw.rd("pe")
                for j2 in range(4):
                    j = 4 * jq + j2
                    for half in range(2):
                      pjo.wrg("sc")
                      for tt8 in range(4 * half, 4 * half + 4):
                        bk = banks[2 + (tt8 % 2)]
                        bk.guard_write("pe")
                        for kc in range(KC):
                            p.op("pe", lambda e, b=bk.ap(),
                                 l=pjw.t[:, 512 * kc + 128 * j2: 512 * kc + 128 * (j2 + 1)],
                                 r=xT[:, kc * S + 512 * tt8: kc * S + 512 * (tt8 + 1)],
                                 st=(kc == 0), sp=(kc == KC - 1):
                                 e.matmul(b, l, r, start=st, stop=sp),
                                 tick=tPE if kc == KC - 1 else None)
                        bk.set_w(tPE, tPE.n)
                        bk.guard_read("sc")
                        ev = p.op("sc", lambda e, b=bk.ap(),
                                  d=pjo.t[:, 512 * (tt8 % 4):512 * (tt8 % 4) + 512]:
                                  e.activation(d, b, AF.Copy), tAC)
                        bk.add_reader(tAC, ev)
                      pjo.set_w(tAC, tAC.n)
                      pjo.dspill("sc", lambda e,
                                 d=dst_T[128 * j:128 * (j + 1), 2048 * half:2048 * (half + 1)],
                                 s=pjo.t: e.dma_start(d, s[:]))
                      dq_pump(2)
                pjw.add_rd(tPE, tPE.n)
        for jq in range(4):
            pjw.wrg("sc")
            load_wquarter(w_in["v"], jq)
            pjw.rd("pe")
            for tc in range(32):
                bk = banks[2 + (tc % 2)]
                bk.guard_write("pe")
                for kc in range(KC):
                    p.op("pe", lambda e, b=bk.ap(),
                         l=xT[:, kc * S + 128 * tc: kc * S + 128 * (tc + 1)],
                         r=pjw.t[:, 512 * kc:512 * (kc + 1)],
                         st=(kc == 0), sp=(kc == KC - 1):
                         e.matmul(b, l, r, start=st, stop=sp),
                         tick=tPE if kc == KC - 1 else None)
                bk.set_w(tPE, tPE.n)
                vo = vout[tc % 2]
                vo.wrg("sc")
                bk.guard_read("sc")
                ev = p.op("sc", lambda e, b=bk.ap(), d=vo.t:
                          e.activation(d[:], b, AF.Copy), tAC)
                bk.add_reader(tAC, ev)
                vo.set_w(tAC, ev)
                vo.dspill("sc", lambda e,
                          d=v_s[128 * tc:128 * (tc + 1), 512 * jq:512 * (jq + 1)],
                          s=vo.t: e.dma_start(d, s[:]))
                dq_pump(2)
            pjw.add_rd(tPE, tPE.n)

    dq_sem_names = set()
    for b_ in (dq_in + dq_sc + dq_wo + [dq_gfL, dq_gfH, dq_m1L, dq_m1H] +
               list(dq_t.values())):
        for sm_ in (b_._qsem, b_._ssem):
            if sm_ is not None:
                dq_sem_names.add(sm_.name)

    def barrier(name):
        for b_ in (dq_in + dq_sc + dq_wo):
            for sm_ in (b_._qsem, b_._ssem):
                if sm_ is not None:
                    dq_sem_names.add(sm_.name)
        for ek in ("sy", "sc", "dv", "pe"):
            for sem in p.all_sems:
                if sem.name in ("tPO",) or sem.name in dq_sem_names:
                    continue
                if sem.n > 0:
                    p.wait(ek, sem, sem.n)
    barrier("p1end")
    sc0.close()

    if want("P2"):
        sc2 = ExitStack()

        def sb2(name, shape, dt):
            return Buf(p, sc2.enter_context(nc.sbuf_tensor(name, shape, dt)))

        qq = sb2("qq", [128, KC * 512], BF16)
        kkb = sb2("kkb", [128, KC * 512], BF16)
        vv = sb2("vv", [128, 4 * D], BF16)
        xseg = sb2("xseg", [128, 4 * D], F32)
        wot = [sb2(f"wot{i}", [128, KC * 128], BF16) for i in range(2)]
        attT = sb2("attT", [128, H * 512], BF16)
        et = sb2("et", [128, 4 * 512], BF16)
        ptb = sb2("ptb", [128, 4 * 512], BF16)
        qsc = sb2("qsc", [128, 512], BF16)
        sq = sb2("sq", [128, 512], BF16)
        sk = sb2("sk", [128, 512], BF16)
        tmpA = sb2("tmpA", [128, 512], F32)
        tmpB = sb2("tmpB", [128, 512], F32)
        sgA = sb2("sgA", [128, 512], F32)
        sgB = sb2("sgB", [128, 512], F32)
        skA = sb2("skA", [128, 512], F32)
        skB = sb2("skB", [128, 512], F32)
        amT = sb2("amT", [128, 512], F32)
        rzq = sb2("rzq", [128, 512], F32)
        mxn = sb2("mxn", [128, 4], F32)
        smv = sb2("smv", [128, 4], F32)
        rsv = sb2("rsv", [128, 4], F32)
        rzk = sb2("rzk", [128, 4], F32)
        delta = sb2("delta", [128, 4 * 128], BF16)
        skn = sb2("skn", [128, 4 * 128], BF16)
        woTmp = sb2("woTmp", [128, 512], F32)
        nsum = sb2("nsum", [128, 4], F32)
        sqsum = sb2("sqsum", [128, 4], F32)
        varv = sb2("varv", [128, 4], F32)
        rstd = sb2("rstd", [128, 4], F32)
        m2t = sb2("m2t", [128, 4], F32)
        sqscr = sb2("sqscr", [128, D], BF16)
        x2b = sb2("x2b", [128, 4 * D], BF16)
        x2Tst = [sb2(f"x2Tst{i}", [128, 512], BF16) for i in range(2)]

        def load_seg(s):
            qq.dload("sy", lambda e, d=qq.t,
                     s_=qT_s[:, SEG * s:SEG * (s + 1)].rearrange("(kc q) n -> q kc n", q=128):
                     e.dma_start(d[:].rearrange("q (kc n) -> q kc n", kc=KC), s_))
            kkb.dload("sy", lambda e, d=kkb.t,
                      s_=kT_s[:, SEG * s:SEG * (s + 1)].rearrange("(kc q) n -> q kc n", q=128):
                      e.dma_start(d[:].rearrange("q (kc n) -> q kc n", kc=KC), s_))
            vv.dload("sy", lambda e, d=vv.t,
                     s_=v_s[SEG * s:SEG * (s + 1), :].rearrange("(c q) n -> q c n", q=128):
                     e.dma_start(d[:].rearrange("q (c n) -> q c n", c=4), s_))

        def load_xseg(s):
            xseg.dload("sy", lambda e, d=xseg.t,
                       s_=x_in[SEG * s:SEG * (s + 1), :].rearrange("(c q) n -> q c n", q=128):
                       e.dma_start(d[:].rearrange("q (c n) -> q c n", c=4), s_))

        if _DBG.get("nheads", H) < H:
            p.op("dv", lambda e: e.memset(attT.t[:], 0.0), tDV)
            attT.set_w(tDV, tDV.n)
        load_seg(0)
        load_xseg(0)
        mem_ready = [0] * H
        zf_ready = [0] * H
        zrep_rd = [None]

        for s in range(_DBG.get("nseg", NSEG)):
            NH = _DBG.get("nheads", H)
            sbanks = [(banks[0], 0), (banks[0], 128), (banks[1], 0), (banks[2], 0)]
            skm_pos = [(banks[7], 0), (banks[7], 132), (banks[7], 264), (banks[6], 0)]

            def head_early(h):
                hs = 512 * h
                qq.rd("dv")
                qsc.wrg("dv")
                tq = p.op("dv", lambda e, o=qsc.t, i=qq.t[:, hs:hs + 512]:
                          e.tensor_scalar(o[:], i, ISQ, None, ALU.mult), tDV)
                qsc.set_w(tDV, tq)
                qq.add_rd(tDV, tq)
                kkb.rd("pe")
                qsc.rd("pe")
                for c in range(4):
                    bk, off = sbanks[c]
                    w = 128 * (c + 1)
                    bk.guard_write("pe")
                    tm = p.op("pe", lambda e, b=bk.ap(), o=off, w_=w,
                              l=qsc.t[:, 128 * c:128 * (c + 1)],
                              r=kkb.t[:, hs:hs + w]:
                              e.matmul(b[:, o:o + w_], l, r, start=True, stop=True), tPE)
                    bk.set_w(tPE, tm)
                qsc.add_rd(tPE, tPE.n)
                kkb.add_rd(tPE, tPE.n)
                qq.rd("dv")
                sgA.wrg("dv")
                p.op("dv", lambda e, i=qq.t[:, hs:hs + 512]: e.tensor_scalar(
                    sgA.t[:], i, 0.0, None, ALU.min), tDV)
                sgA.set_w(tDV, tDV.n)
                qq.add_rd(tDV, tDV.n)
                sgA.rd("sc")
                sgB.wrg("sc")
                tex = p.op("sc", lambda e: e.activation(sgB.t[:], sgA.t[:], AF.Exp), tAC)
                sgB.set_w(tAC, tex)
                sgA.add_rd(tAC, tex)
                kkb.rd("dv")
                skA.wrg("dv")
                p.op("dv", lambda e, i=kkb.t[:, hs:hs + 512]: e.tensor_scalar(
                    skA.t[:], i, 0.0, None, ALU.min), tDV)
                skA.set_w(tDV, tDV.n)
                kkb.add_rd(tDV, tDV.n)
                skA.rd("sc")
                skB.wrg("sc")
                tex = p.op("sc", lambda e: e.activation(skB.t[:], skA.t[:], AF.Exp), tAC)
                skB.set_w(tAC, tex)
                skA.add_rd(tAC, tex)

            def head_mid(h):
                hs = 512 * h
                for c in range(4):
                    bk, off = sbanks[c]
                    bk.guard_read("dv")
                    tmsk = p.op("dv", lambda e, b=bk.ap(), o=off, c_=c:
                                e.tensor_tensor(b[:, o + 128 * c_:o + 128 * (c_ + 1)],
                                                b[:, o + 128 * c_:o + 128 * (c_ + 1)],
                                                maskT[:], ALU.add), tDV)
                    bk.set_w(tDV, tmsk)
                for c in range(4):
                    bk, off = sbanks[c]
                    w = 128 * (c + 1)
                    bk.guard_read("dv")
                    tmx = p.op("dv", lambda e, b=bk.ap(), o=off, w_=w, c_=c:
                               e.tensor_reduce(mxn.t[:, c_:c_ + 1], b[:, o:o + w_],
                                               AX.X, ALU.max, negate=True), tDV)
                    mxn.set_w(tDV, tmx)
                    bk.add_reader(tDV, tmx)
                qq.rd("dv")
                tmpA.wrg("dv")
                p.op("dv", lambda e, i=qq.t[:, hs:hs + 512]: e.tensor_scalar(
                    tmpA.t[:], i, 0.0, None, ALU.max), tDV)
                tmpA.set_w(tDV, tDV.n)
                qq.add_rd(tDV, tDV.n)
                tmpA.rd("dv")
                sgB.rd("dv")
                sq.wrg("dv")
                tsq = p.op("dv", lambda e: e.tensor_tensor(
                    sq.t[:], sgB.t[:], tmpA.t[:], ALU.add), tDV)
                sq.set_w(tDV, tsq)
                tmpA.add_rd(tDV, tsq)
                sgB.add_rd(tDV, tsq)
                kkb.rd("dv")
                tmpB.wrg("dv")
                p.op("dv", lambda e, i=kkb.t[:, hs:hs + 512]: e.tensor_scalar(
                    tmpB.t[:], i, 0.0, None, ALU.max), tDV)
                tmpB.set_w(tDV, tDV.n)
                kkb.add_rd(tDV, tDV.n)
                tmpB.rd("dv")
                skB.rd("dv")
                sk.wrg("dv")
                tsk = p.op("dv", lambda e: e.tensor_tensor(
                    sk.t[:], skB.t[:], tmpB.t[:], ALU.add), tDV)
                sk.set_w(tDV, tsk)
                tmpB.add_rd(tDV, tsk)
                skB.add_rd(tDV, tsk)
                et.wrg("sc")
                for c in range(4):
                    bk, off = sbanks[c]
                    w = 128 * (c + 1)
                    bk.guard_read("sc")
                    p.wait("sc", tDV, mxn.w[1])
                    tex2 = p.op("sc", lambda e, b=bk.ap(), o=off, w_=w, c_=c:
                                e.activation(et.t[:, 512 * c_:512 * c_ + w_], b[:, o:o + w_],
                                             AF.Exp, bias=mxn.t[:, c_:c_ + 1],
                                             accum_out=smv.t[:, c_:c_ + 1]), tAC)
                    et.set_w(tAC, tex2)
                    smv.set_w(tAC, tex2)
                    bk.add_reader(tAC, tex2)
                bk5, bk6 = banks[5], banks[6]
                bk5.guard_write("pe")
                sq.rd("pe")
                p.wait("pe", tDV, t_dv_init)
                p.wait("pe", tAC, mem_ready[h])
                tm = p.op("pe", lambda e, b=bk5.ap(), h_=h:
                          e.matmul(b, memb[:, MEMW * h_:MEMW * h_ + 128], sq.t[:],
                                   start=True, stop=True), tPE)
                bk5.set_w(tPE, tm)
                p.wait("dv", tDV, zf_ready[h])
                if zrep_rd[0] is not None:
                    p.wait("dv", zrep_rd[0][0], zrep_rd[0][1])
                zrw = p.op("dv", lambda e, h_=h: e.tensor_scalar(
                    zrepb[:], onesb[:], zf[:, h_:h_ + 1], None, ALU.mult), tDV)
                bk6.guard_write("pe")
                p.wait("pe", tDV, zrw)
                tm6 = p.op("pe", lambda e, b=bk6.ap():
                           e.matmul(b, zrepb[:], sq.t[:], start=True, stop=True), tPE)
                bk6.set_w(tPE, tm6)
                zrep_rd[0] = (tPE, tm6)
                sq.add_rd(tPE, tPE.n)
                sk.rd("pe")
                for c in range(3):
                    bk, off = skm_pos[c]
                    bk.guard_write("pe")
                    tm = p.op("pe", lambda e, b=bk.ap(), o=off, c_=c, h_=h:
                              e.matmul(b[:, o:o + 132],
                                       sk.t[:, 128 * c_:128 * (c_ + 1)],
                                       memb[:, MEMW * h_:MEMW * h_ + 132],
                                       start=True, stop=True), tPE)
                    bk.set_w(tPE, tm)
                sk.add_rd(tPE, tPE.n)

            def head_late(h):
                hs = 512 * h
                bk5, bk6 = banks[5], banks[6]
                bk4 = banks[4]
                smv.rd("dv")
                rsv.wrg("dv")
                trs = p.op("dv", lambda e: e.reciprocal(rsv.t[:], smv.t[:]), tDV)
                rsv.set_w(tDV, trs)
                smv.add_rd(tDV, trs)
                et.rd("dv")
                rsv.rd("dv")
                tp_ = None
                for c in range(4):
                    w = 128 * (c + 1)
                    tp_ = p.op("dv", lambda e, c_=c, w_=w: e.tensor_scalar(
                        et.t[:, 512 * c_:512 * c_ + w_], et.t[:, 512 * c_:512 * c_ + w_],
                        rsv.t[:, c_:c_ + 1], None, ALU.mult), tDV)
                et.set_w(tDV, tp_)
                bk6.guard_read("dv")
                rzq.wrg("dv")
                trz = p.op("dv", lambda e, b=bk6.ap(): e.reciprocal(rzq.t[:], b), tDV)
                rzq.set_w(tDV, trz)
                bk6.add_reader(tDV, trz)
                bk5.guard_read("dv")
                rzq.rd("dv")
                amT.wrg("dv")
                tam = p.op("dv", lambda e, b=bk5.ap(): e.tensor_tensor(
                    amT.t[:], b, rzq.t[:], ALU.mult), tDV)
                amT.set_w(tDV, tam)
                bk5.add_reader(tDV, tam)
                rzq.add_rd(tDV, tam)
                bk, off = skm_pos[3]
                bk.guard_write("pe")
                sk.rd("pe")
                tm = p.op("pe", lambda e, b=bk.ap(), o=off, h_=h:
                          e.matmul(b[:, o:o + 132], sk.t[:, 384:512],
                                   memb[:, MEMW * h_:MEMW * h_ + 132],
                                   start=True, stop=True), tPE)
                bk.set_w(tPE, tm)
                et.rd("pe")
                for j in range(4):
                    bk3 = banks[3]
                    bk3.guard_write("pe")
                    for c in range(j, 4):
                        tt = p.op("pe", lambda e, b=bk3.bf(), c_=c, j_=j:
                                  e.transpose(b[:, 128 * c_:128 * (c_ + 1)],
                                              et.t[:, 512 * c_ + 128 * j_:512 * c_ + 128 * (j_ + 1)],
                                              idbT[:]), tPE)
                    bk3.set_w(tPE, tPE.n)
                    bk3.guard_read("dv")
                    ptb.wrg("dv") if j == 0 else None
                    ev = p.op("dv", lambda e, b=bk3.bf(), j_=j:
                              e.tensor_copy(ptb.t[:, 512 * j_ + 128 * j_:512 * j_ + 512],
                                            b[:, 128 * j_:512]), tDV)
                    bk3.add_reader(tDV, ev)
                et.add_rd(tPE, tPE.n)
                ptb.set_w(tDV, tDV.n)
                bk4.guard_write("pe")
                ptb.rd("pe")
                vv.rd("pe")
                for j in range(4):
                    tm = p.op("pe", lambda e, b=bk4.ap(), j_=j, h_=h:
                              e.matmul(b[:, 128 * j_:512],
                                       vv.t[:, D * j_ + 128 * h_:D * j_ + 128 * (h_ + 1)],
                                       ptb.t[:, 512 * j_ + 128 * j_:512 * j_ + 512],
                                       start=(j_ == 0), stop=(j_ == 3)), tPE)
                bk4.set_w(tPE, tPE.n)
                ptb.add_rd(tPE, tPE.n)
                vv.add_rd(tPE, tPE.n)
                rzk.wrg("dv")
                delta.wrg("dv")
                vv.rd("dv")
                tlast = None
                for c in range(4):
                    bk, off = skm_pos[c]
                    bk.guard_read("dv")
                    p.op("dv", lambda e, b=bk.ap(), o=off, c_=c:
                         e.reciprocal(rzk.t[:, c_:c_ + 1], b[:, o + 128:o + 129]), tDV)
                    tlast = p.op("dv", lambda e, b=bk.ap(), o=off, c_=c, h_=h:
                                 e.scalar_tensor_tensor(
                                     delta.t[:, 128 * c_:128 * (c_ + 1)],
                                     b[:, o:o + 128], rzk.t[:, c_:c_ + 1],
                                     vv.t[:, D * c_ + 128 * h_:D * c_ + 128 * (h_ + 1)],
                                     ALU.mult, ALU.subtract), tDV)
                    bk.add_reader(tDV, tlast)
                rzk.set_w(tDV, tlast)
                delta.set_w(tDV, tlast)
                vv.add_rd(tDV, tlast)
                sk.rd("pe")
                skn.wrg("dv")
                for c in range(4):
                    bk3 = banks[3]
                    bk3.guard_write("pe")
                    tt = p.op("pe", lambda e, b=bk3.bf(), c_=c:
                              e.transpose(b[:, 0:128],
                                          sk.t[:, 128 * c_:128 * (c_ + 1)], idbT[:]), tPE)
                    bk3.set_w(tPE, tt)
                    bk3.guard_read("dv")
                    ev = p.op("dv", lambda e, b=bk3.bf(), c_=c:
                              e.tensor_copy(skn.t[:, 128 * c_:128 * (c_ + 1)],
                                            b[:, 0:128]), tDV)
                    bk3.add_reader(tDV, ev)
                sk.add_rd(tPE, tPE.n)
                skn.set_w(tDV, tDV.n)
                bk3 = banks[3]
                bk3.guard_write("pe")
                skn.rd("pe")
                delta.rd("pe")
                for c in range(4):
                    tm = p.op("pe", lambda e, b=bk3.ap(), c_=c:
                              e.matmul(b[:, 0:128],
                                       skn.t[:, 128 * c_:128 * (c_ + 1)],
                                       delta.t[:, 128 * c_:128 * (c_ + 1)],
                                       start=(c_ == 0), stop=(c_ == 3)), tPE)
                bk3.set_w(tPE, tPE.n)
                skn.add_rd(tPE, tPE.n)
                delta.add_rd(tPE, tPE.n)
                bk3.guard_read("dv")
                tmu = p.op("dv", lambda e, b=bk3.ap(), h_=h: e.tensor_tensor(
                    memf[:, MEMW * h_:MEMW * h_ + 128],
                    memf[:, MEMW * h_:MEMW * h_ + 128], b[:, 0:128], ALU.subtract), tDV)
                bk3.add_reader(tDV, tmu)
                sk.rd("dv")
                tzs = p.op("dv", lambda e: e.tensor_reduce(
                    tmpA.t[:, 0:1], sk.t[:], AX.X, ALU.add), tDV)
                tmpA.set_w(tDV, tzs)
                sk.add_rd(tDV, tzs)
                tzf = p.op("dv", lambda e, h_=h: e.tensor_tensor(
                    zf[:, h_:h_ + 1], zf[:, h_:h_ + 1], tmpA.t[:, 0:1], ALU.add), tDV)
                tmpA.add_rd(tDV, tzf)
                p.wait("sc", tDV, tmu)
                tmb = p.op("sc", lambda e, h_=h: e.activation(
                    memb[:, MEMW * h_:MEMW * h_ + 128],
                    memf[:, MEMW * h_:MEMW * h_ + 128], AF.Copy), tAC)
                p.wait("sc", tDV, tzf)
                tmz = p.op("sc", lambda e, h_=h: e.activation(
                    memb[:, MEMW * h_ + 128:MEMW * h_ + 129],
                    zf[:, h_:h_ + 1], AF.Copy), tAC)
                mem_ready[h] = tmz
                zf_ready[h] = tzf
                amT.rd("dv")
                tmpB.wrg("dv")
                tb1 = p.op("dv", lambda e, h_=h: e.tensor_scalar(
                    tmpB.t[:], amT.t[:], bT[:, h_:h_ + 1], None, ALU.mult), tDV)
                tmpB.set_w(tDV, tb1)
                amT.add_rd(tDV, tb1)
                bk4.guard_read("dv")
                tmpB.rd("dv")
                tat = p.op("dv", lambda e, b=bk4.ap(), h_=h: e.scalar_tensor_tensor(
                    attT.t[:, 512 * h_:512 * (h_ + 1)], b,
                    bTo[:, h_:h_ + 1], tmpB.t[:], ALU.mult, ALU.add), tDV)
                bk4.add_reader(tDV, tat)
                tmpB.add_rd(tDV, tat)
                attT.set_w(tDV, tat)
                dq_pump(1)

            head_early(0)
            head_mid(0)
            for h in range(NH):
                if h + 1 < NH:
                    head_early(h + 1)
                head_late(h)
                if h + 1 < NH:
                    head_mid(h + 1)
            if s + 1 < _DBG.get("nseg", NSEG):
                load_seg(s + 1)
            attT.rd("pe")
            xseg.rd("dv")
            for mc in range(16):
                wt = wot[mc % 2]
                if mc == 0 and s == 0:
                    for wb in wbst_for_gate:
                        if wb._ssem is not None:
                            p.wait("sy", wb._ssem, wb._ssem.n)
                wt.dload("sy", lambda e, d=wt.t, mc_=mc:
                         e.dma_start(d[:].rearrange("q (kc m) -> q kc m", kc=KC),
                                     wo_s2[2048 * mc_:2048 * (mc_ + 1), :]
                                     .rearrange("(kc q) m -> q kc m", q=128)))
                bkw = banks[4 + (mc % 2)]
                bkw.guard_write("pe")
                wt.rd("pe")
                attR = attT.t[:].rearrange("p (a b c) -> p a b c", a=16, b=32)
                for jv in range(16):
                    tm = p.op("pe", lambda e, b=bkw.ap(), jv_=jv, w_=wt.t,
                              r_=attR[:, :, :, jv]:
                              e.matmul(b, w_[:, 128 * jv_:128 * (jv_ + 1)],
                                       r_, start=(jv_ == 0), stop=(jv_ == 15)), tPE)
                bkw.set_w(tPE, tPE.n)
                wt.add_rd(tPE, tPE.n)
                bkw.guard_read("dv")
                woTmp.wrg("dv")
                tev = p.op("dv", lambda e, b=bkw.ap(): e.tensor_copy(woTmp.t[:], b), tDV)
                woTmp.set_w(tDV, tev)
                bkw.add_reader(tDV, tev)
                woTmp.rd("pe")
                for c in range(4):
                    bkt = banks[6 + (c % 2)]
                    bkt.guard_write("pe")
                    tt = p.op("pe", lambda e, b=bkt.ap(), c_=c:
                              e.transpose(b[:, 0:128],
                                          woTmp.t[:, 128 * c_:128 * (c_ + 1)], idfT[:]), tPE)
                    bkt.set_w(tPE, tt)
                    bkt.guard_read("dv")
                    tev2 = p.op("dv", lambda e, b=bkt.ap(), c_=c, mc_=mc:
                                e.tensor_tensor(xseg.t[:, D * c_ + 128 * mc_:D * c_ + 128 * (mc_ + 1)],
                                                xseg.t[:, D * c_ + 128 * mc_:D * c_ + 128 * (mc_ + 1)],
                                                b[:, 0:128], ALU.add), tDV)
                    bkt.add_reader(tDV, tev2)
                woTmp.add_rd(tPE, tPE.n)
                xseg.set_w(tDV, tDV.n)
            xseg.rd("dv")
            xseg.rd("sc")
            for c in range(4):
                tns = p.op("dv", lambda e, c_=c: e.tensor_reduce(
                    nsum.t[:, c_:c_ + 1], xseg.t[:, D * c_:D * (c_ + 1)],
                    AX.X, ALU.add, negate=True), tDV)
                nsum.set_w(tDV, tns)
                sqscr.wrg("sc")
                tsq_ = p.op("sc", lambda e, c_=c: e.activation(
                    sqscr.t[:], xseg.t[:, D * c_:D * (c_ + 1)], AF.Square,
                    accum_out=sqsum.t[:, c_:c_ + 1]), tAC)
                sqscr.set_w(tAC, tsq_)
                sqsum.set_w(tAC, tsq_)
            xseg.add_rd(tAC, tAC.n)
            nsum.rd("dv")
            tnm = p.op("dv", lambda e: e.tensor_scalar(
                nsum.t[:], nsum.t[:], 1.0 / D, None, ALU.mult), tDV)
            nsum.set_w(tDV, tnm)
            tm2 = p.op("dv", lambda e: e.tensor_tensor(
                m2t.t[:], nsum.t[:], nsum.t[:], ALU.mult), tDV)
            m2t.set_w(tDV, tm2)
            p.wait("dv", tAC, tAC.n)
            tvv = p.op("dv", lambda e: e.scalar_tensor_tensor(
                varv.t[:], sqsum.t[:], 1.0 / D, m2t.t[:], ALU.mult, ALU.subtract), tDV)
            varv.set_w(tDV, tvv)
            sqsum.add_rd(tDV, tvv)
            m2t.add_rd(tDV, tvv)
            varv.rd("sc")
            trsd = p.op("sc", lambda e: e.activation(
                rstd.t[:], varv.t[:], AF.Sqrt, bias=epsT[:, 0:1]), tAC)
            rstd.set_w(tAC, trsd)
            varv.add_rd(tAC, trsd)
            rstd.rd("dv")
            trsd2 = p.op("dv", lambda e: e.reciprocal(rstd.t[:], rstd.t[:]), tDV)
            rstd.set_w(tDV, trsd2)
            rstd.rd("dv")
            nsum.rd("dv")
            x2b.wrg("sc")
            for c in range(4):
                tx2 = p.op("dv", lambda e, c_=c: e.tensor_scalar(
                    xseg.t[:, D * c_:D * (c_ + 1)], xseg.t[:, D * c_:D * (c_ + 1)],
                    nsum.t[:, c_:c_ + 1], rstd.t[:, c_:c_ + 1], ALU.add, ALU.mult), tDV)
                xseg.set_w(tDV, tx2)
                xseg.dspill("sc", lambda e, c_=c, s_=s:
                            e.dma_start(x2_s[SEG * s_ + 128 * c_:SEG * s_ + 128 * (c_ + 1), :],
                                        xseg.t[:, D * c_:D * (c_ + 1)]))
                tcv = p.op("sc", lambda e, c_=c: e.activation(
                    x2b.t[:, D * c_:D * (c_ + 1)], xseg.t[:, D * c_:D * (c_ + 1)],
                    AF.Copy), tAC)
                xseg.add_rd(tAC, tcv)
            x2b.set_w(tAC, tAC.n)
            nsum.add_rd(tDV, tDV.n)
            rstd.add_rd(tDV, tDV.n)
            x2b.rd("pe")
            for kc in range(KC):
                stg = x2Tst[kc % 2]
                stg.wrg("dv")
                blk, eo = divmod(kc, 2)
                for c in range(4):
                    bkt = banks[6 + (c % 2)]
                    bkt.guard_write("pe")
                    tt = p.op("pe", lambda e, b=bkt.bf(), c_=c, bl=blk, e_=eo:
                              e.transpose(b[:, 0:128],
                                          x2b.t[:, D * c_ + 256 * bl + e_:D * c_ + 256 * (bl + 1):2],
                                          idbT[:]), tPE)
                    bkt.set_w(tPE, tt)
                    bkt.guard_read("dv")
                    ev = p.op("dv", lambda e, b=bkt.bf(), c_=c, sg_=stg.t:
                              e.tensor_copy(sg_[:, 128 * c_:128 * (c_ + 1)], b[:, 0:128]), tDV)
                    bkt.add_reader(tDV, ev)
                stg.set_w(tDV, tDV.n)
                stg.dspill("sc", lambda e, kc_=kc, s_=s, d=x2T_s, st_=stg.t:
                           e.dma_start(d[128 * kc_:128 * (kc_ + 1), SEG * s_:SEG * (s_ + 1)],
                                       st_[:]))
            x2b.add_rd(tPE, tPE.n)
            if s + 1 < _DBG.get("nseg", NSEG):
                load_xseg(s + 1)
        sc2.close()
        barrier("p2end")

    if want("P4"):
        sc4 = ExitStack()

        def sb4(name, shape, dt):
            return Buf(p, sc4.enter_context(nc.sbuf_tensor(name, shape, dt)))

        x2te = sb4("x2te", [128, KC * 512], BF16)
        inter = sb4("inter", [128, 16 * 512], BF16)
        accb = sb4("accb", [128, 16 * 512], F32)
        wds = sb4("wds", [128, 8 * D], BF16)
        wgb = [sb4(f"wgb{i}", [128, KC * 256], BF16) for i in range(2)]
        wub = [sb4(f"wub{i}", [128, KC * 256], BF16) for i in range(2)]
        gq = sb4("gq", [128, 512], F32)
        gr = sb4("gr", [128, 512], F32)
        gt_ = gq
        gw_ = sb4("gww", [128, 512], F32)
        gA = gw_
        x2nat = [sb4("x2nat0", [128, D], F32)]
        natt = [sb4(f"natt{i}", [128, D], F32) for i in range(2)]
        nsum4 = sb4("nsum4", [128, 1], F32)
        sqsum4 = sb4("sqsum4", [128, 1], F32)
        m2t4 = sb4("m2t4", [128, 1], F32)
        varv4 = sb4("varv4", [128, 1], F32)
        rstd4 = sb4("rstd4", [128, 1], F32)
        sqscr4 = inter

        NTE = 8
        for te in range(NTE):
            x2te.dload("sy", lambda e, te_=te, d=x2te.t:
                       e.dma_start(d[:].rearrange("q (kc n) -> q kc n", kc=KC),
                                   x2T_s[:, 512 * te_:512 * (te_ + 1)]
                                   .rearrange("(kc q) n -> q kc n", q=128)))
            for sp_ in range(8):
                dq_pump(96)
                for sm_, v_ in dqs["group_tok"].get(sp_, []):
                    p.wait("sy", sm_, v_)
                for bl in range(8):
                    if te == 0:
                        dq_pump(2)
                    if bl == 0:
                        wds.dload("sy", lambda e, sp2=sp_, d=wds.t:
                                  e.dma_start(d[:].rearrange("q (kc n) -> q kc n", kc=8),
                                              wd_s[2048 * sp2:2048 * sp2 + 1024, :]
                                              .rearrange("(kc q) n -> q kc n", q=128)))
                    wg2 = wgb[bl % 2]
                    wu2 = wub[bl % 2]
                    wg2.dload("sy", lambda e, sp2=sp_, bl_=bl, d=wg2.t:
                              e.dma_start(d[:].rearrange("q (kc n) -> q kc n", kc=KC),
                                          wg_s[:, 2048 * sp2 + 256 * bl_:2048 * sp2 + 256 * (bl_ + 1)]
                                          .rearrange("(kc q) n -> q kc n", q=128)))
                    wu2.dload("sy", lambda e, sp2=sp_, bl_=bl, d=wu2.t:
                              e.dma_start(d[:].rearrange("q (kc n) -> q kc n", kc=KC),
                                          wu_s[:, 2048 * sp2 + 256 * bl_:2048 * sp2 + 256 * (bl_ + 1)]
                                          .rearrange("(kc q) n -> q kc n", q=128)))
                    for eo in range(2):
                        jl = 2 * bl + eo
                        bkg = banks[jl % 2]
                        bku = banks[2 + (jl % 2)]
                        bkg.guard_write("pe")
                        bku.guard_write("pe")
                        wg2.rd("pe")
                        wu2.rd("pe")
                        x2te.rd("pe")
                        for kc in range(KC):
                            p.op("pe", lambda e, b=bkg.ap(), kc_=kc, e_=eo, w_=wg2.t:
                                 e.matmul(b, w_[:, 256 * kc_ + e_:256 * (kc_ + 1):2],
                                          x2te.t[:, 512 * kc_:512 * (kc_ + 1)],
                                          start=(kc_ == 0), stop=(kc_ == KC - 1)),
                                 tick=tPE if kc == KC - 1 else None)
                        bkg.set_w(tPE, tPE.n)
                        for kc in range(KC):
                            p.op("pe", lambda e, b=bku.ap(), kc_=kc, e_=eo, w_=wu2.t:
                                 e.matmul(b, w_[:, 256 * kc_ + e_:256 * (kc_ + 1):2],
                                          x2te.t[:, 512 * kc_:512 * (kc_ + 1)],
                                          start=(kc_ == 0), stop=(kc_ == KC - 1)),
                                 tick=tPE if kc == KC - 1 else None)
                        bku.set_w(tPE, tPE.n)
                        wg2.add_rd(tPE, tPE.n)
                        wu2.add_rd(tPE, tPE.n)
                        bkg.guard_read("sc")
                        gq.wrg("sc")
                        t_q = p.op("sc", lambda e, b=bkg.ap(): e.activation(
                            gq.t[:], b, AF.Square, scale=0.5), tAC)
                        gq.set_w(tAC, t_q)
                        gq.rd("dv")
                        gr.wrg("dv")
                        t_r = p.op("dv", lambda e: e.tensor_scalar(
                            gr.t[:], gq.t[:], 0.044715, 1.0, ALU.mult, ALU.add), tDV)
                        gr.set_w(tDV, t_r)
                        gq.add_rd(tDV, t_r)
                        bkg.guard_read("dv")
                        gr.rd("dv")
                        gt_.wrg("dv")
                        t_t = p.op("dv", lambda e, b=bkg.ap(): e.scalar_tensor_tensor(
                            gt_.t[:], b, 0.5 * 0.7978845608028654, gr.t[:],
                            ALU.mult, ALU.mult), tDV)
                        gt_.set_w(tDV, t_t)
                        gr.add_rd(tDV, t_t)
                        gt_.rd("sc")
                        gw_.wrg("sc")
                        t_w = p.op("sc", lambda e: e.activation(
                            gw_.t[:], gt_.t[:], AF.Tanh), tAC)
                        gw_.set_w(tAC, t_w)
                        gt_.add_rd(tAC, t_w)
                        gw_.rd("dv")
                        gA.wrg("dv")
                        t_A = p.op("dv", lambda e, b=bkg.ap(): e.scalar_tensor_tensor(
                            gA.t[:], gw_.t[:], 1.0, b, ALU.add, ALU.mult), tDV)
                        gA.set_w(tDV, t_A)
                        gw_.add_rd(tDV, t_A)
                        bkg.add_reader(tDV, t_A)
                        bku.guard_read("dv")
                        gA.rd("dv")
                        t_i = p.op("dv", lambda e, b=bku.ap(), jl_=jl:
                                   e.scalar_tensor_tensor(
                                       inter.t[:, 512 * jl_:512 * (jl_ + 1)],
                                       gA.t[:], 0.125, b, ALU.mult, ALU.mult), tDV)
                        gA.add_rd(tDV, t_i)
                        bku.add_reader(tDV, t_i)
                        inter.set_w(tDV, tDV.n)
                inter.rd("pe")
                for hf in range(2):
                    if hf == 1:
                        wds.dload("sy", lambda e, sp2=sp_, d=wds.t:
                                  e.dma_start(d[:].rearrange("q (kc n) -> q kc n", kc=8),
                                              wd_s[2048 * sp2 + 1024:2048 * sp2 + 2048, :]
                                              .rearrange("(kc q) n -> q kc n", q=128)))
                    wds.rd("pe")
                    for hc in range(16):
                        bkd = banks[4 + (hc % 2)]
                        bkd.guard_write("pe")
                        for jl in range(8):
                            jg = 8 * hf + jl
                            p.op("pe", lambda e, b=bkd.ap(), jl_=jl, jg_=jg, hc_=hc:
                                 e.matmul(b, wds.t[:, D * jl_ + 128 * hc_:D * jl_ + 128 * (hc_ + 1)],
                                          inter.t[:, 512 * jg_:512 * (jg_ + 1)],
                                          start=(jl_ == 0), stop=(jl_ == 7)),
                                 tick=tPE if jl == 7 else None)
                        bkd.set_w(tPE, tPE.n)
                        if sp_ == 0 and hf == 0:
                            bkd.guard_read("sc")
                            accb.wrg("sc")
                            tacc = p.op("sc", lambda e, b=bkd.ap(), hc_=hc:
                                        e.activation(accb.t[:, 512 * hc_:512 * (hc_ + 1)],
                                                     b, AF.Copy), tAC)
                            accb.set_w(tAC, tacc)
                            bkd.add_reader(tAC, tacc)
                        else:
                            bkd.guard_read("dv")
                            accb.rd("dv")
                            tacc = p.op("dv", lambda e, b=bkd.ap(), hc_=hc:
                                        e.tensor_tensor(
                                            accb.t[:, 512 * hc_:512 * (hc_ + 1)],
                                            accb.t[:, 512 * hc_:512 * (hc_ + 1)],
                                            b, ALU.add), tDV)
                            accb.set_w(tDV, tacc)
                            bkd.add_reader(tDV, tacc)
                    wds.add_rd(tPE, tPE.n)
                inter.add_rd(tPE, tPE.n)
            accb.rd("pe")
            for tc4 in range(4):
                xn = x2nat[0]
                nt = natt[tc4 % 2]
                xn.dload("sy", lambda e, te_=te, tc_=tc4, d=xn.t:
                         e.dma_start(d[:], x2_s[512 * te_ + 128 * tc_:512 * te_ + 128 * (tc_ + 1), :]))
                nt.wrg("dv")
                xn.rd("dv")
                for hc in range(16):
                    bkt = banks[6 + (hc % 2)]
                    bkt.guard_write("pe")
                    tt = p.op("pe", lambda e, b=bkt.ap(), hc_=hc, tc_=tc4:
                              e.transpose(b[:, 0:128],
                                          accb.t[:, 512 * hc_ + 128 * tc_:512 * hc_ + 128 * (tc_ + 1)],
                                          idfT[:]), tPE)
                    bkt.set_w(tPE, tt)
                    bkt.guard_read("dv")
                    ev = p.op("dv", lambda e, b=bkt.ap(), hc_=hc, n_=nt.t, x_=xn.t:
                              e.scalar_tensor_tensor(
                                  n_[:, 128 * hc_:128 * (hc_ + 1)],
                                  b[:, 0:128], 0.5,
                                  x_[:, 128 * hc_:128 * (hc_ + 1)],
                                  ALU.mult, ALU.add), tDV)
                    bkt.add_reader(tDV, ev)
                nt.set_w(tDV, tDV.n)
                xn.add_rd(tDV, tDV.n)
                nt.rd("dv")
                nt.rd("sc")
                tns = p.op("dv", lambda e, n_=nt.t: e.tensor_reduce(
                    nsum4.t[:], n_[:], AX.X, ALU.add, negate=True), tDV)
                nsum4.set_w(tDV, tns)
                sqscr4.wrg("sc")
                tsq_ = p.op("sc", lambda e, n_=nt.t: e.activation(
                    sqscr4.t[:, 0:D].bitcast(BF16)[:, 0:D], n_[:], AF.Square,
                    accum_out=sqsum4.t[:]), tAC)
                sqscr4.set_w(tAC, tsq_)
                sqsum4.set_w(tAC, tsq_)
                nt.add_rd(tAC, tsq_)
                nsum4.rd("dv")
                tnm = p.op("dv", lambda e: e.tensor_scalar(
                    nsum4.t[:], nsum4.t[:], 1.0 / D, None, ALU.mult), tDV)
                nsum4.set_w(tDV, tnm)
                tm2 = p.op("dv", lambda e: e.tensor_tensor(
                    m2t4.t[:], nsum4.t[:], nsum4.t[:], ALU.mult), tDV)
                m2t4.set_w(tDV, tm2)
                p.wait("dv", tAC, tsq_)
                tvv = p.op("dv", lambda e: e.scalar_tensor_tensor(
                    varv4.t[:], sqsum4.t[:], 1.0 / D, m2t4.t[:], ALU.mult, ALU.subtract), tDV)
                varv4.set_w(tDV, tvv)
                sqsum4.add_rd(tDV, tvv)
                m2t4.add_rd(tDV, tvv)
                varv4.rd("sc")
                trsd = p.op("sc", lambda e: e.activation(
                    rstd4.t[:], varv4.t[:], AF.Sqrt, bias=epsT[:, 0:1]), tAC)
                rstd4.set_w(tAC, trsd)
                varv4.add_rd(tAC, trsd)
                rstd4.rd("dv")
                trsd2 = p.op("dv", lambda e: e.reciprocal(rstd4.t[:], rstd4.t[:]), tDV)
                rstd4.set_w(tDV, trsd2)
                rstd4.rd("dv")
                nsum4.rd("dv")
                tno = p.op("dv", lambda e, n_=nt.t: e.tensor_scalar(
                    n_[:], n_[:], nsum4.t[:], rstd4.t[:], ALU.add, ALU.mult), tDV)
                nt.set_w(tDV, tno)
                nsum4.add_rd(tDV, tno)
                rstd4.add_rd(tDV, tno)
                nt.dspill("po", lambda e, te_=te, tc_=tc4, n_=nt.t:
                          e.dma_start(out_d[512 * te_ + 128 * tc_:512 * te_ + 128 * (tc_ + 1), :],
                                      n_[:]))
            accb.add_rd(tPE, tPE.n)
        sc4.close()
    else:
        dq_pump(10 ** 6)
        zf0 = sbuf("zf0", [128, D], F32)
        t0_ = p.op("dv", lambda e: e.memset(zf0[:], 0.0), tDV)
        p.wait("po", tDV, t0_)
        for tc in range(32):
            p.dma("po", lambda e, tc_=tc: e.dma_start(
                out_d[128 * tc_:128 * (tc_ + 1), :], zf0[:]), qOut)

    for sm_ in p.all_sems:
        if sm_.n > 0 and sm_.name[0] in "bq":
            p.wait("po", sm_, sm_.n)
    p.wait("po", qOut, qOut.n)

    with nc.Block() as block:
        @block.sync
        def _(eng):
            for f in p.rec["sy"]:
                f(eng)

        @block.scalar
        def _(eng):
            for f in p.rec["sc"]:
                f(eng)

        @block.vector
        def _(eng):
            for f in p.rec["dv"]:
                f(eng)

        @block.tensor
        def _(eng):
            for f in p.rec["pe"]:
                f(eng)

        @block.gpsimd
        def _(eng):
            for f in p.rec["po"]:
                f(eng)

    es.close()
    return nc


def kernel(**inputs):
    maps = _host_prep(inputs)
    nc = build_program()
    res = run_bass_kernel_spmd(nc, maps, core_ids=list(range(8)))
    return np.stack([np.asarray(r["out"]) for r in res.results], axis=0).astype(np.float32)


def kernel(**inputs):
    maps = _host_prep(inputs)
    nc = build_program()
    res = run_bass_kernel_spmd(nc, maps, core_ids=list(range(8)))
    return np.stack([np.asarray(r["out"]) for r in res.results], axis=0).astype(np.float32)
